# version 01
# baseline (speedup 1.0000x reference)
import numpy as np
import concourse.bass as bass
import concourse.mybir as mybir
from concourse.alu_op_type import AluOpType
from concourse.bass_utils import run_bass_kernel_spmd

F32 = mybir.dt.float32
BF16 = mybir.dt.bfloat16
U32 = mybir.dt.uint32
AF = mybir.ActivationFunctionType
AX = mybir.AxisListType
AL = AluOpType

NCORES = 8
B = 2
T = 16384
D = 1024
DI = 2048
NST = 16
TSL = T // 4
HALO = 3
XW = TSL + 2 * HALO + 2


class Emitter:

    ENGINES = ("sync", "tensor", "vector", "scalar", "gpsimd")

    def __init__(self, nc):
        self.nc = nc
        self.q = {e: [] for e in self.ENGINES}
        self._sems = {}
        self._ctxs = []

    def sem(self, name):
        if name not in self._sems:
            ctx = self.nc.semaphore(name)
            h = ctx.__enter__()
            self._ctxs.append(ctx)
            self._sems[name] = [h, 0]
        return name

    def val(self, name):
        return self._sems[name][1]

    def emit(self, eng, fn, inc=None, amt=1, waits=()):
        waits = list(waits)
        if eng == "vector":
            inc = inc or "s_v"
            waits.append(("s_v", self.val("s_v")))
        elif eng == "scalar":
            inc = inc or "s_s"
            waits.append(("s_s", self.val("s_s")))
        wl = [(self._sems[s][0], v) for s, v in waits if v > 0]
        h = None
        if inc is not None:
            self.sem(inc)
            self._sems[inc][1] += amt
            h = self._sems[inc][0]

        def run(e, fn=fn, wl=wl, h=h, amt=amt):
            for hh, v in wl:
                e.wait_ge(hh, v)
            r = fn(e)
            if h is not None:
                r.then_inc(h, amt)
        self.q[eng].append(run)

    def wait(self, eng, sem, value=None):
        v = self.val(sem) if value is None else value
        if v > 0:
            h = self._sems[sem][0]
            self.q[eng].append(lambda e, h=h, v=v: e.wait_ge(h, v))

    def finalize(self, block):
        for name in self.ENGINES:
            items = self.q[name]
            if not items:
                continue
            sec = getattr(block, name)

            def body(e, items=items):
                for it in items:
                    it(e)
            sec(body)

    def close(self):
        for c in reversed(self._ctxs):
            c.__exit__(None, None, None)


S_GP = "s_gp"
S_SY = "s_sy"
S_T = "s_t"
S_V = "s_v"
S_S = "s_s"


def build_p1():
    nc = bass.Bass("TRN2", target_bir_lowering=False, debug=False)
    xT = nc.dram_tensor("xT", [D, XW], F32, kind="ExternalInput").ap()
    w_inT = nc.dram_tensor("w_inT", [D, 2 * DI], F32, kind="ExternalInput").ap()
    cw = {d: nc.dram_tensor(f"cw_{d}", [128, 64], F32, kind="ExternalInput").ap() for d in "fb"}
    cb = {d: nc.dram_tensor(f"cb_{d}", [128, 16], F32, kind="ExternalInput").ap() for d in "fb"}
    xpjT = {d: nc.dram_tensor(f"xpjT_{d}", [128, 1536], F32, kind="ExternalInput").ap() for d in "fb"}
    dtwT = {d: nc.dram_tensor(f"dtwT_{d}", [64, DI], F32, kind="ExternalInput").ap() for d in "fb"}
    dtb = {d: nc.dram_tensor(f"dtb_{d}", [128, 16], F32, kind="ExternalInput").ap() for d in "fb"}

    u = {d: nc.dram_tensor(f"u_{d}", [DI, TSL], F32, kind="ExternalOutput").ap() for d in "fb"}
    delta = {d: nc.dram_tensor(f"delta_{d}", [DI, TSL], F32, kind="ExternalOutput").ap() for d in "fb"}
    zsil = nc.dram_tensor("zsil", [DI, TSL], F32, kind="ExternalOutput").ap()
    Bout = {d: nc.dram_tensor(f"B_{d}", [NST, TSL], F32, kind="ExternalOutput").ap() for d in "fb"}
    Cout = {d: nc.dram_tensor(f"C_{d}", [NST, TSL], F32, kind="ExternalOutput").ap() for d in "fb"}

    ftiles = []
    off = 0
    while off < XW:
        fw = min(512, XW - off)
        ftiles.append((off, fw))
        off += fw
    NFT = TSL // 512

    import contextlib
    with contextlib.ExitStack() as stack:
        sb = lambda name, shape, dt: stack.enter_context(nc.sbuf_tensor(name, shape, dt))
        psm = lambda name, shape: stack.enter_context(nc.psum_tensor(name, shape, F32))
        xbf = sb("xbf", [128, 8 * XW], BF16)
        wsl = sb("wsl", [128, 2 * 2048], BF16)
        xz = sb("xz", [128, XW], F32)
        xc = sb("xc", [128, TSL], F32)
        xcb = sb("xcb", [128, TSL], BF16)
        xdbl_f = sb("xdbl_f", [96, TSL], F32)
        xdbl_b = sb("xdbl_b", [96, TSL], F32)
        xdbf = sb("xdbf", [64, TSL], BF16)
        zst = sb("zstb", [128, 2048], F32)
        tmp = sb("tmpb", [128, 2048], F32)
        cwsb = sb("cwsb", [128, 2 * 64], F32)
        cbsb = sb("cbsb", [128, 2 * 16], F32)
        xpsb = sb("xpsb", [128, 2 * 16 * 96], BF16)
        dtwsb = sb("dtwsb", [64, 2 * 2048], BF16)
        dtbsb = sb("dtbsb", [128, 2 * 16], F32)
        psA = psm("psA", [128, 512])
        psB = psm("psB", [128, 512])
        psX = psm("psX", [96, 512])
        psY = psm("psY", [96, 512])
        block = stack.enter_context(nc.Block())
        E = Emitter(nc)
        for s in (S_GP, S_SY, S_T, S_V, S_S, "s_ini", "s_w0", "s_w1"):
            E.sem(s)
        xdbl = {"f": xdbl_f, "b": xdbl_b}

        E.emit("gpsimd", lambda g: g.dma_start(
            out=xbf.ap().rearrange("p (kc t) -> p kc t", kc=8),
            in_=xT.rearrange("(kc p) t -> p kc t", p=128)), inc="s_ini", amt=16)
        for di, d in enumerate("fb"):
            E.emit("gpsimd", lambda g, d=d, di=di: g.dma_start(
                out=cwsb.ap()[:, di * 64:(di + 1) * 64],
                in_=cw[d]), inc="s_ini", amt=16)
            E.emit("gpsimd", lambda g, d=d, di=di: g.dma_start(
                out=cbsb.ap()[:, di * 16:(di + 1) * 16],
                in_=cb[d]), inc="s_ini", amt=16)
            E.emit("gpsimd", lambda g, d=d, di=di: g.dma_start(
                out=xpsb.ap()[:, di * 1536:(di + 1) * 1536],
                in_=xpjT[d]), inc="s_ini", amt=16)
            E.emit("gpsimd", lambda g, d=d, di=di: g.dma_start(
                out=dtwsb.ap()[0:64, di * 2048:(di + 1) * 2048],
                in_=dtwT[d]), inc="s_ini", amt=16)
            E.emit("gpsimd", lambda g, d=d, di=di: g.dma_start(
                out=dtbsb.ap()[:, di * 16:(di + 1) * 16],
                in_=dtb[d]), inc="s_ini", amt=16)

        mm_done = {}

        def load_wslice(gi, buf):
            col0 = gi * 256
            waits = []
            if gi >= 2:
                waits.append((S_T, mm_done[gi - 2]))
            E.emit("gpsimd", lambda g, col0=col0, buf=buf: g.dma_start(
                out=wsl.ap()[:, buf * 2048:(buf + 1) * 2048].rearrange("p (kc m) -> p kc m", kc=8),
                in_=w_inT[:, col0:col0 + 256].rearrange("(kc p) m -> p kc m", p=128)),
                inc=f"s_w{buf}", amt=16, waits=waits)
            wload_gp[gi] = (f"s_w{buf}", E.val(f"s_w{buf}"))

        wload_gp = {}
        load_wslice(0, 0)

        bank_free = {id(psA): 0, id(psB): 0, id(psX): 0, id(psY): 0}

        for gi in range(16):
            is_x = gi < 8
            if gi + 1 < 16:
                load_wslice(gi + 1, (gi + 1) % 2)
            buf = gi % 2
            for m in range(2):
                ch0 = (gi % 8) * 256 + m * 128
                for fi, (fo, fw) in enumerate(ftiles):
                    ps = psA if fi % 2 == 0 else psB
                    waits = [(S_S, bank_free[id(ps)])]
                    if fi == 0 and m == 0:
                        waits.append(wload_gp[gi])
                        waits.append(("s_ini", E.val("s_ini")))
                    for k in range(8):
                        col = buf * 2048 + k * 256 + m * 128
                        E.emit("tensor", lambda t, ps=ps, col=col, k=k, fo=fo, fw=fw:
                               t.matmul(ps[:, 0:fw], wsl[:, col:col + 128],
                                        xbf[:, k * XW + fo:k * XW + fo + fw],
                                        start=(k == 0), stop=(k == 7)),
                               waits=(waits if k == 0 else ()),
                               inc=(S_T if k == 7 else None))
                    if m == 1 and fi == len(ftiles) - 1:
                        mm_done[gi] = E.val(S_T)
                    ew = [(S_T, E.val(S_T))]
                    if fi == 0:
                        ew.append((S_V, E.val(S_V)))
                    E.emit("scalar", lambda s, ps=ps, fo=fo, fw=fw:
                           s.activation(out=xz[:, fo:fo + fw], in_=ps[:, 0:fw], func=AF.Copy),
                           inc=S_S, waits=ew)
                    bank_free[id(ps)] = E.val(S_S)
                evac_all = E.val(S_S)

                if is_x:
                    for di, d in enumerate("fb"):
                        for k in range(4):
                            o = k if d == "f" else (6 - k)
                            cwcol = cwsb[:, di * 64 + (gi * 2 + m) * 4 + k:
                                         di * 64 + (gi * 2 + m) * 4 + k + 1]
                            cbcol = cbsb[:, di * 16 + gi * 2 + m: di * 16 + gi * 2 + m + 1]
                            src = xz[:, o:o + TSL]
                            if k == 0:
                                E.emit("vector", lambda v, src=src, cwcol=cwcol, cbcol=cbcol:
                                       v.tensor_scalar(xc[:, :], src, cwcol, cbcol,
                                                       AL.mult, AL.add),
                                       inc=S_V, waits=[(S_S, evac_all), (S_SY, E.val(S_SY))])
                            else:
                                E.emit("vector", lambda v, src=src, cwcol=cwcol:
                                       v.scalar_tensor_tensor(xc[:, :], src, cwcol,
                                                              xc[:, :], AL.mult, AL.add),
                                       inc=S_V)
                        E.emit("vector", lambda v: v.tensor_copy(xcb[:, :], xc[:, :]), inc=S_V)
                        cast_v = E.val(S_V)
                        xa = xdbl[d]
                        first = (gi == 0 and m == 0)
                        wslice = xpsb[:, di * 1536 + (2 * gi + m) * 96:
                                      di * 1536 + (2 * gi + m) * 96 + 96]
                        for fi in range(NFT):
                            fo = fi * 512
                            ps = psX if fi % 2 == 0 else psY
                            mw = [(S_V, cast_v if fi == 0 else 0),
                                  (S_V, bank_free[id(ps)])]
                            E.emit("tensor", lambda t, fo=fo, ps=ps, wslice=wslice:
                                   t.matmul(ps[:, :], wslice,
                                            xcb[:, fo:fo + 512], start=True, stop=True),
                                   waits=[w for w in mw if w[1] > 0], inc=S_T)
                            if first:
                                E.emit("vector", lambda v, xa=xa, fo=fo, ps=ps:
                                       v.tensor_copy(xa[:, fo:fo + 512], ps[:, :]),
                                       inc=S_V, waits=[(S_T, E.val(S_T))])
                            else:
                                E.emit("vector", lambda v, xa=xa, fo=fo, ps=ps:
                                       v.scalar_tensor_tensor(xa[:, fo:fo + 512], ps[:, :], 0.0,
                                                              xa[:, fo:fo + 512], AL.add, AL.add),
                                       inc=S_V, waits=[(S_T, E.val(S_T))])
                            bank_free[id(ps)] = E.val(S_V)
                        E.emit("vector", lambda v:
                               v.tensor_scalar(xc[:, :], xc[:, :], 10.0, -10.0, AL.min, AL.max),
                               inc=S_V)
                        E.emit("sync", lambda s, d=d, ch0=ch0: s.dma_start(
                            out=u[d][ch0:ch0 + 128, :], in_=xc[:, :]),
                            inc=S_SY, amt=16, waits=[(S_V, E.val(S_V))])
                else:
                    E.emit("scalar", lambda s:
                           s.activation(out=xc[:, :], in_=xz[:, HALO:HALO + TSL], func=AF.Silu),
                           inc=S_S, waits=[(S_SY, E.val(S_SY))])
                    E.emit("sync", lambda s, ch0=ch0: s.dma_start(
                        out=zsil[ch0:ch0 + 128, :], in_=xc[:, :]),
                        inc=S_SY, amt=16, waits=[(S_S, E.val(S_S))])

        for di, d in enumerate("fb"):
            E.emit("vector", lambda v, d=d: v.tensor_copy(
                xdbf[0:64, 0:TSL], xdbl[d][0:64, :]),
                inc=S_V, waits=[(S_T, E.val(S_T))] if di > 0 else ())
            cast_v = E.val(S_V)
            for mi in range(16):
                for hf in range(2):
                    hof = hf * 2048
                    for fi in range(4):
                        ps = psA if fi % 2 == 0 else psB
                        fo = hof + fi * 512
                        waits = [(S_V, max(bank_free[id(ps)], cast_v))]
                        if mi == 0 and hf == 0 and fi == 0:
                            waits.append((S_S, E.val(S_S)))
                        wslice = dtwsb[0:64, di * 2048 + mi * 128: di * 2048 + mi * 128 + 128]
                        E.emit("tensor", lambda t, fo=fo, ps=ps, wslice=wslice:
                               t.matmul(ps[:, :], wslice, xdbf[0:64, fo:fo + 512],
                                        start=True, stop=True),
                               waits=[w for w in waits if w[1] > 0], inc=S_T)
                        dtbcol = dtbsb[:, di * 16 + mi: di * 16 + mi + 1]
                        ew = [(S_T, E.val(S_T))]
                        if fi == 0:
                            ew.append((S_S, E.val(S_S)))
                        E.emit("vector", lambda v, ps=ps, zo=fi * 512, dtbcol=dtbcol:
                               v.tensor_scalar(zst[:, zo:zo + 512], ps[:, :], dtbcol, 1.0,
                                               AL.add, AL.min),
                               inc=S_V, waits=ew)
                        bank_free[id(ps)] = E.val(S_V)
                    dtbcol = dtbsb[:, di * 16 + mi: di * 16 + mi + 1]
                    E.emit("vector", lambda v, dtbcol=dtbcol:
                           v.tensor_scalar(zst[:, :], zst[:, :], 1e-5, dtbcol,
                                           AL.max, AL.add), inc=S_V)
                    zv = E.val(S_V)
                    E.emit("scalar", lambda s: s.activation(out=tmp[:, :], in_=zst[:, :], func=AF.Abs),
                           inc=S_S, waits=[(S_V, zv), (S_SY, E.val(S_SY))])
                    E.emit("scalar", lambda s: s.activation(out=tmp[:, :], in_=tmp[:, :], func=AF.Exp, scale=-1.0),
                           inc=S_S)
                    E.emit("scalar", lambda s: s.activation(out=tmp[:, :], in_=tmp[:, :], func=AF.Ln, bias=1.0),
                           inc=S_S)
                    E.emit("vector", lambda v:
                           v.scalar_tensor_tensor(tmp[:, :], zst[:, :], 0.0, tmp[:, :],
                                                  AL.max, AL.add),
                           inc=S_V, waits=[(S_S, E.val(S_S))])
                    E.emit("sync", lambda s, d=d, mi=mi, hof=hof: s.dma_start(
                        out=delta[d][mi * 128:(mi + 1) * 128, hof:hof + 2048], in_=tmp[:, :]),
                        inc=S_SY, amt=16, waits=[(S_V, E.val(S_V))])
            lv = E.val(S_V)
            E.emit("sync", lambda s, d=d: s.dma_start(out=Bout[d][:, :], in_=xdbl[d][64:80, :]),
                   inc=S_SY, amt=16, waits=[(S_V, lv)])
            E.emit("sync", lambda s, d=d: s.dma_start(out=Cout[d][:, :], in_=xdbl[d][80:96, :]),
                   inc=S_SY, amt=16)

        E.wait("gpsimd", S_SY, E.val(S_SY))
        E.finalize(block)
        E.close()
    return nc


def _swz(a):
    a = np.asarray(a, np.float32)
    m = a.shape[0] // 128
    return np.ascontiguousarray(a.reshape(m, 128, -1).transpose(1, 0, 2).reshape(128, -1))


def _p1_inputs_for_core(c, inputs, xT_full):
    b, sl = divmod(c, 4)
    t0 = sl * TSL
    xh = np.zeros((D, XW), np.float32)
    lo, hi = max(0, t0 - HALO), min(T, t0 + TSL + HALO)
    xh[:, (lo - t0) + HALO: (hi - t0) + HALO] = xT_full[:, b, lo:hi]
    im = {
        "xT": xh,
        "w_inT": np.ascontiguousarray(inputs["L0_attn_in_proj_w"].T),
        "cw_f": _swz(inputs["L0_attn_conv1d_w"][:, 0, :]),
        "cw_b": _swz(inputs["L0_attn_conv1d_b_w"][:, 0, :]),
        "cb_f": np.ascontiguousarray(np.asarray(inputs["L0_attn_conv1d_bias"], np.float32).reshape(16, 128).T),
        "cb_b": np.ascontiguousarray(np.asarray(inputs["L0_attn_conv1d_b_bias"], np.float32).reshape(16, 128).T),
        "xpjT_f": _swz(np.asarray(inputs["L0_attn_x_proj_w"], np.float32).T),
        "xpjT_b": _swz(np.asarray(inputs["L0_attn_x_proj_b_w"], np.float32).T),
        "dtwT_f": np.ascontiguousarray(inputs["L0_attn_dt_proj_w"].T),
        "dtwT_b": np.ascontiguousarray(inputs["L0_attn_dt_proj_b_w"].T),
        "dtb_f": np.ascontiguousarray(np.asarray(inputs["L0_attn_dt_proj_bias"], np.float32).reshape(16, 128).T),
        "dtb_b": np.ascontiguousarray(np.asarray(inputs["L0_attn_dt_proj_b_bias"], np.float32).reshape(16, 128).T),
    }
    return im


def run_p1(inputs):
    x0 = np.asarray(inputs["L0_attn_hidden_states"], np.float32)
    xT_full = np.ascontiguousarray(x0.transpose(2, 0, 1))
    nc = build_p1()
    ims = [_p1_inputs_for_core(c, inputs, xT_full) for c in range(NCORES)]
    res = run_bass_kernel_spmd(nc, ims, core_ids=list(range(NCORES)))
    out = {}
    for name, rows in (("u_f", DI), ("u_b", DI), ("delta_f", DI), ("delta_b", DI),
                       ("zsil", DI), ("B_f", NST), ("B_b", NST), ("C_f", NST), ("C_b", NST)):
        full = np.empty((rows, B, T), np.float32)
        for c in range(NCORES):
            b, sl = divmod(c, 4)
            full[:, b, sl * TSL:(sl + 1) * TSL] = res.results[c][name]
        out[name] = full
    return out




def rsl(t, c0, c1):
    if c0 == 0:
        return t[:, c1 - 1::-1]
    return t[:, c1 - 1:c0 - 1:-1]


def build_p2(nb=B, ndt=2, dirs="fb", ntc=None, nst=NST):
    nc = bass.Bass("TRN2", target_bir_lowering=False, debug=False)
    CH = 2048
    NTC = T // CH
    uin = {d: nc.dram_tensor(f"u_{d}", [256, B, T], F32, kind="ExternalInput").ap() for d in "fb"}
    din = {d: nc.dram_tensor(f"delta_{d}", [256, B, T], F32, kind="ExternalInput").ap() for d in "fb"}
    Bin = {d: nc.dram_tensor(f"B_{d}", [NST, B, T], F32, kind="ExternalInput").ap() for d in "fb"}
    Cin = {d: nc.dram_tensor(f"C_{d}", [NST, B, T], F32, kind="ExternalInput").ap() for d in "fb"}
    alog = {d: nc.dram_tensor(f"alog_{d}", [128, 32], F32, kind="ExternalInput").ap() for d in "fb"}
    dvec = nc.dram_tensor("dvec", [128, 4], F32, kind="ExternalInput").ap()
    ident = nc.dram_tensor("ident", [128, 128], F32, kind="ExternalInput").ap()
    yout = nc.dram_tensor("y", [256, B, T], F32, kind="ExternalOutput").ap()

    import contextlib
    with contextlib.ExitStack() as stack:
        sb = lambda name, shape, dt: stack.enter_context(nc.sbuf_tensor(name, shape, dt))
        dlt = sb("dlt", [128, 2 * CH], F32)
        uu = sb("uu", [128, 2 * CH], F32)
        wch = sb("wch", [128, CH], F32)
        udc = sb("udc", [128, CH], BF16)
        dab = sb("dab", [128, 2 * CH], F32)
        bbb = sb("bbb", [128, 2 * CH], F32)
        hts = sb("hts", [128, 16 * CH], BF16)
        hcc = sb("hcc", [128, 2 * CH], BF16)
        brep = sb("brep", [128, 2 * CH], F32)
        crep = sb("crep", [128, 2 * CH], F32)
        evb = sb("evb", [128, CH], F32)
        isb = sb("isb", [128, 128], BF16)
        asb = sb("asb", [128, 2 * 32], F32)
        dsb = sb("dsb", [128, 4], F32)
        y_ps = stack.enter_context(nc.psum_tensor("y_ps", [128, CH], F32))
        block = stack.enter_context(nc.Block())
        E = Emitter(nc)
        for s in (S_T, S_V, S_S, "s_ini", "s_du0", "s_du1", "s_br0", "s_br1",
                  "s_cr0", "s_cr1", "s_yo"):
            E.sem(s)

        E.emit("gpsimd", lambda g: g.dma_start(out=isb.ap(), in_=ident), inc="s_ini", amt=16)
        E.emit("gpsimd", lambda g: g.dma_start(out=dsb.ap(), in_=dvec), inc="s_ini", amt=16)
        for di, d in enumerate("fb"):
            E.emit("gpsimd", lambda g, d=d, di=di: g.dma_start(
                out=asb.ap()[:, di * 32:(di + 1) * 32], in_=alog[d]), inc="s_ini", amt=16)
        ini = E.val("s_ini")
        E.emit("scalar", lambda s: s.activation(out=asb[:, :], in_=asb[:, :], func=AF.Exp),
               waits=[("s_ini", ini)])
        E.emit("vector", lambda v: v.tensor_scalar(asb[:, :], asb[:, :], -1.0, None, AL.mult),
               waits=[(S_S, E.val(S_S))])
        aready_s, aready_v = E.val(S_S), E.val(S_V)
        E.wait("scalar", S_V, aready_v)
        slot_cons = {}
        rep_cons = {}

        for b in range(nb):
            for dtile in range(ndt):
                for di, d in enumerate(dirs):
                    rev = (d == "b")
                    yf_start = E.val("s_yo")
                    for tc in range(ntc or NTC):
                        ttc = (NTC - 1 - tc) if rev else tc
                        toff = ttc * CH
                        slot = tc % 2
                        sdu = f"s_du{slot}"
                        lw = []
                        if slot in slot_cons:
                            pv = slot_cons[slot]
                            lw = [(S_S, pv[0]), (S_V, pv[1])]
                        E.emit("sync", lambda g, d=d, b=b, dtile=dtile, toff=toff, slot=slot:
                               g.dma_start(out=dlt.ap()[:, slot * CH:(slot + 1) * CH],
                                           in_=din[d][dtile * 128:(dtile + 1) * 128, b, toff:toff + CH]),
                               inc=sdu, amt=16, waits=lw)
                        E.emit("sync", lambda g, d=d, b=b, dtile=dtile, toff=toff, slot=slot:
                               g.dma_start(out=uu.ap()[:, slot * CH:(slot + 1) * CH],
                                           in_=uin[d][dtile * 128:(dtile + 1) * 128, b, toff:toff + CH]),
                               inc=sdu, amt=16)
                        duv = E.val(sdu)
                        dslice = rsl(dlt, slot * CH, (slot + 1) * CH) if rev else dlt[:, slot * CH:(slot + 1) * CH]
                        uslice = rsl(uu, slot * CH, (slot + 1) * CH) if rev else uu[:, slot * CH:(slot + 1) * CH]
                        E.emit("vector", lambda v, dslice=dslice, uslice=uslice:
                               v.tensor_tensor(wch[:, :], dslice, uslice, AL.mult),
                               waits=[(sdu, duv)])
                        dcol = dsb[:, di * 2 + dtile: di * 2 + dtile + 1]
                        E.emit("vector", lambda v, uslice=uslice, dcol=dcol:
                               v.tensor_scalar(udc[:, :], uslice, dcol, None, AL.mult))
                        uD_v = E.val(S_V)
                        mm_w = [(S_V, uD_v), (S_S, E.val(S_S)), ("s_ini", ini)]
                        for j in range(CH // 512):
                            E.emit("tensor", lambda t, j=j:
                                   t.matmul(y_ps[:, j * 512:(j + 1) * 512], isb[:, :],
                                            udc[:, j * 512:(j + 1) * 512], start=True, stop=False),
                                   waits=(mm_w if j == 0 else ()), inc=(S_T if j == CH // 512 - 1 else None))
                        for n in range(nst):
                            ns = n % 2
                            acol = asb[:, di * 32 + dtile * 16 + n: di * 32 + dtile * 16 + n + 1]
                            dw = [] if n else [(sdu, duv)]
                            dw = dw + [(S_V, slot_cons.get(("dab", ns), 0))]
                            E.emit("scalar", lambda s, acol=acol, dslice=dslice, ns=ns:
                                   s.activation(out=dab[:, ns * CH:(ns + 1) * CH], in_=dslice,
                                                func=AF.Exp, scale=acol),
                                   waits=dw)
                            da_s = E.val(S_S)
                            sbr = f"s_br{ns}"
                            bw = [(S_V, rep_cons.get(("b", ns), 0))]
                            E.emit("gpsimd", lambda g, d=d, b=b, toff=toff, n=n, ns=ns:
                                   g.dma_start(out=brep.ap()[:, ns * CH:(ns + 1) * CH],
                                               in_=Bin[d][n:n + 1, b, toff:toff + CH]
                                               .broadcast_to([128, CH])),
                                   inc=sbr, amt=16, waits=bw)
                            brv = E.val(sbr)
                            scr = f"s_cr{ns}"
                            cwt = [(S_V, rep_cons.get(("c", ns), 0))]
                            E.emit("gpsimd", lambda g, d=d, b=b, toff=toff, n=n, ns=ns:
                                   g.dma_start(out=crep.ap()[:, ns * CH:(ns + 1) * CH],
                                               in_=Cin[d][n:n + 1, b, toff:toff + CH]
                                               .broadcast_to([128, CH])),
                                   inc=scr, amt=16, waits=cwt)
                            crv = E.val(scr)
                            bslice = (rsl(brep, ns * CH, (ns + 1) * CH) if rev
                                      else brep[:, ns * CH:(ns + 1) * CH])
                            E.emit("vector", lambda v, bslice=bslice, ns=ns:
                                   v.tensor_tensor(bbb[:, ns * CH:(ns + 1) * CH], wch[:, :],
                                                   bslice, AL.mult),
                                   waits=[(sbr, brv)])
                            rep_cons[("b", ns)] = E.val(S_V)
                            hsl = hts[:, n * CH:(n + 1) * CH]
                            if tc == 0:
                                E.emit("vector", lambda v, hsl=hsl, ns=ns:
                                       v.tensor_tensor_scan(out=hsl, data0=dab[:, ns * CH:(ns + 1) * CH],
                                                            data1=bbb[:, ns * CH:(ns + 1) * CH],
                                                            initial=0.0, op0=AL.mult, op1=AL.add),
                                       waits=[(S_S, da_s)])
                            else:
                                ival = hts[:, n * CH + CH - 1: n * CH + CH]
                                E.emit("vector", lambda v, hsl=hsl, ns=ns, ival=ival:
                                       v.tensor_tensor_scan(out=hsl, data0=dab[:, ns * CH:(ns + 1) * CH],
                                                            data1=bbb[:, ns * CH:(ns + 1) * CH],
                                                            initial=ival, op0=AL.mult, op1=AL.add),
                                       waits=[(S_S, da_s)])
                            slot_cons[("dab", ns)] = E.val(S_V)
                            cslice = (rsl(crep, ns * CH, (ns + 1) * CH) if rev
                                      else crep[:, ns * CH:(ns + 1) * CH])
                            hw = [(scr, crv), (S_T, slot_cons.get(("hcc", ns), 0))]
                            E.emit("vector", lambda v, hsl=hsl, cslice=cslice, ns=ns:
                                   v.tensor_tensor(hcc[:, ns * CH:(ns + 1) * CH], hsl, cslice, AL.mult),
                                   waits=hw)
                            rep_cons[("c", ns)] = E.val(S_V)
                            hc_v = E.val(S_V)
                            for j in range(CH // 512):
                                E.emit("tensor", lambda t, j=j, ns=ns, n=n:
                                       t.matmul(y_ps[:, j * 512:(j + 1) * 512], isb[:, :],
                                                hcc[:, ns * CH + j * 512: ns * CH + (j + 1) * 512],
                                                start=False, stop=(n == nst - 1)),
                                       waits=([(S_V, hc_v)] if j == 0 else ()),
                                       inc=(S_T if j == CH // 512 - 1 else None))
                            slot_cons[("hcc", ns)] = E.val(S_T)
                        slot_cons[slot] = (E.val(S_S), E.val(S_V))
                        mmv = E.val(S_T)
                        ysl = rsl(evb, 0, CH) if rev else evb[:, :]
                        E.emit("scalar", lambda s, ysl=ysl:
                               s.activation(out=ysl, in_=y_ps[:, :], func=AF.Copy),
                               waits=[(S_T, mmv), ("s_yo", E.val("s_yo"))])
                        ev_s = E.val(S_S)
                        out_tc = (NTC - 1 - tc) if rev else tc
                        row0 = dtile * 128
                        if not rev:
                            E.emit("gpsimd", lambda g, row0=row0, b=b, out_tc=out_tc:
                                   g.dma_start(out=yout[row0:row0 + 128, b, out_tc * CH:(out_tc + 1) * CH],
                                               in_=evb[:, :]),
                                   inc="s_yo", amt=16, waits=[(S_S, ev_s)])
                        else:
                            E.emit("gpsimd", lambda g, row0=row0, b=b, out_tc=out_tc:
                                   g.dma_start(out=yout[row0:row0 + 128, b, out_tc * CH:(out_tc + 1) * CH],
                                               in_=evb[:, :], accum_op=AL.add),
                                   inc="s_yo", amt=16, waits=[(S_S, ev_s), ("s_yo", yf_start)])
        E.wait("gpsimd", "s_yo", E.val("s_yo"))
        E.finalize(block)
        E.close()
    return nc


def run_p2(p1out, inputs):
    CH = 2048
    nc = build_p2()
    ims = []
    for c in range(NCORES):
        r0 = c * 256
        im = {}
        for d in "fb":
            im[f"u_{d}"] = np.ascontiguousarray(p1out[f"u_{d}"][r0:r0 + 256])
            im[f"delta_{d}"] = np.ascontiguousarray(p1out[f"delta_{d}"][r0:r0 + 256])
            im[f"B_{d}"] = p1out[f"B_{d}"]
            im[f"C_{d}"] = p1out[f"C_{d}"]
            key = "L0_attn_A_log" if d == "f" else "L0_attn_A_b_log"
            al = np.asarray(inputs[key], np.float32)[r0:r0 + 256]
            im[f"alog_{d}"] = np.ascontiguousarray(
                al.reshape(2, 128, 16).transpose(1, 0, 2).reshape(128, 32))
        dv = np.empty((128, 4), np.float32)
        dv[:, 0:2] = np.asarray(inputs["L0_attn_D"], np.float32)[r0:r0 + 256].reshape(2, 128).T
        dv[:, 2:4] = np.asarray(inputs["L0_attn_D_b"], np.float32)[r0:r0 + 256].reshape(2, 128).T
        im["dvec"] = dv
        im["ident"] = np.eye(128, dtype=np.float32)
        ims.append(im)
    res = run_bass_kernel_spmd(nc, ims, core_ids=list(range(NCORES)))
    y = np.empty((DI, B, T), np.float32)
    for c in range(NCORES):
        y[c * 256:(c + 1) * 256] = res.results[c]["y"]
    return y


def build_p3():
    nc = bass.Bass("TRN2", target_bir_lowering=False, debug=False)
    NTT = TSL // 128
    yin = nc.dram_tensor("y", [DI, TSL], F32, kind="ExternalInput").ap()
    zin = nc.dram_tensor("zsil", [DI, TSL], F32, kind="ExternalInput").ap()
    xin = nc.dram_tensor("xTr", [D, TSL], F32, kind="ExternalInput").ap()
    wout = nc.dram_tensor("woutTs", [128, 16 * 1024], F32, kind="ExternalInput").ap()
    wrt = nc.dram_tensor("w_routs", [128, 8 * 8], F32, kind="ExternalInput").ap()
    iot = nc.dram_tensor("iota8", [128, 8], F32, kind="ExternalInput").ap()
    x1o = nc.dram_tensor("x1T", [D, TSL], F32, kind="ExternalOutput").ap()
    rto = {k: nc.dram_tensor(k, [128, NTT], F32, kind="ExternalOutput").ap()
           for k in ("idx0", "idx1", "g0", "g1")}

    import contextlib
    with contextlib.ExitStack() as stack:
        sb = lambda name, shape, dt: stack.enter_context(nc.sbuf_tensor(name, shape, dt))
        wob = sb("wob", [128, 16 * 1024], BF16)
        wrb = sb("wrb", [128, 8 * 8], F32)
        iob = sb("iob", [128, 8], F32)
        onecol = sb("onecol", [128, 1], F32)
        onerow = sb("onerow", [1, 128], F32)
        ytl = sb("ytl", [128, 2 * 512], F32)
        ztl = sb("ztl", [128, 2 * 512], F32)
        ygbf = sb("ygbf", [128, 16 * 512], BF16)
        xr = sb("xr", [128, 8 * 512], F32)
        rsb = sb("rsb", [128, 8 * 512], F32)
        rqb = sb("rqb", [128, 8 * 512], F32)
        x1b = sb("x1b", [128, 8 * 512], F32)
        epscol = sb("epscol", [1, 1], F32)
        mrow = sb("mrow", [1, 512], F32)
        vrow = sb("vrow", [1, 512], F32)
        arow = sb("arow", [1, 512], F32)
        brow = sb("brow", [1, 512], F32)
        prb = sb("prb", [128, 8], F32)
        pm8 = sb("pm8", [128, 8], F32)
        mx1 = sb("mx1", [128, 1], F32)
        mx8 = sb("mx8", [128, 8], F32)
        ixu = sb("ixu", [128, 8], U32)
        ixf = sb("ixf", [128, 1], F32)
        msk = sb("msk", [128, 8], F32)
        rstg = {k: sb("st_" + k, [128, NTT], F32) for k in ("idx0", "idx1", "g0", "g1")}
        psO = stack.enter_context(nc.psum_tensor("psO", [128, 512], F32))
        psP = stack.enter_context(nc.psum_tensor("psP", [128, 512], F32))
        psR = stack.enter_context(nc.psum_tensor("psR", [1, 512], F32))
        psQ = stack.enter_context(nc.psum_tensor("psQ", [1, 512], F32))
        psM = stack.enter_context(nc.psum_tensor("psM", [128, 512], F32))
        psN = stack.enter_context(nc.psum_tensor("psN", [128, 512], F32))
        psL = stack.enter_context(nc.psum_tensor("psL", [128, 8], F32))
        block = stack.enter_context(nc.Block())
        E = Emitter(nc)
        for s in (S_T, S_V, S_S, "s_ini", "s_y0", "s_y1", "s_out"):
            E.sem(s)

        E.emit("gpsimd", lambda g: g.dma_start(out=wob.ap(), in_=wout), inc="s_ini", amt=16)
        E.emit("gpsimd", lambda g: g.dma_start(out=wrb.ap(), in_=wrt), inc="s_ini", amt=16)
        E.emit("gpsimd", lambda g: g.dma_start(out=iob.ap(), in_=iot), inc="s_ini", amt=16)
        ini = E.val("s_ini")
        E.emit("vector", lambda v: v.memset(onecol[:, :], 1.0))
        E.emit("vector", lambda v: v.memset(onerow[:, :], 1.0))
        E.emit("vector", lambda v: v.memset(epscol[:, :], 1e-5))
        onev = E.val(S_V)

        bank_free = {id(psO): (S_S, 0), id(psP): (S_S, 0)}
        psl_free = 0
        slot_cons = {}
        for ft in range(8):
            fo = ft * 512
            for kc in range(16):
                slot = kc % 2
                sem = f"s_y{slot}"
                lw = [slot_cons[slot]] if slot in slot_cons else []
                E.emit("sync", lambda g, kc=kc, fo=fo, slot=slot: g.dma_start(
                    out=ytl.ap()[:, slot * 512:(slot + 1) * 512],
                    in_=yin[kc * 128:(kc + 1) * 128, fo:fo + 512]), inc=sem, amt=16, waits=lw)
                E.emit("sync", lambda g, kc=kc, fo=fo, slot=slot: g.dma_start(
                    out=ztl.ap()[:, slot * 512:(slot + 1) * 512],
                    in_=zin[kc * 128:(kc + 1) * 128, fo:fo + 512]), inc=sem, amt=16)
                E.emit("vector", lambda v, kc=kc, slot=slot: v.tensor_tensor(
                    ygbf[:, kc * 512:(kc + 1) * 512], ytl[:, slot * 512:(slot + 1) * 512],
                    ztl[:, slot * 512:(slot + 1) * 512], AL.mult),
                    waits=[(sem, E.val(sem))])
                slot_cons[slot] = (S_V, E.val(S_V))
            ygv = E.val(S_V)
            E.emit("sync", lambda g, fo=fo: g.dma_start(
                out=xr.ap().rearrange("p (c f) -> p c f", c=8),
                in_=xin[:, fo:fo + 512].rearrange("(c p) f -> p c f", p=128)),
                inc="s_out", amt=16, waits=[("s_out", E.val("s_out"))])
            xrv = E.val("s_out")
            for mt in range(8):
                ps = psO if mt % 2 == 0 else psP
                for kc in range(16):
                    w = [bank_free[id(ps)], (S_V, ygv if kc == 0 else 0),
                         ("s_ini", ini if (ft == 0 and mt == 0 and kc == 0) else 0)]
                    E.emit("tensor", lambda t, ps=ps, kc=kc, mt=mt:
                           t.matmul(ps[:, :], wob[:, kc * 1024 + mt * 128: kc * 1024 + mt * 128 + 128],
                                    ygbf[:, kc * 512:(kc + 1) * 512],
                                    start=(kc == 0), stop=(kc == 15)),
                           waits=[x for x in w if x[1] > 0], inc=(S_T if kc == 15 else None))
                E.emit("vector", lambda v, ps=ps, mt=mt:
                       v.scalar_tensor_tensor(rsb[:, mt * 512:(mt + 1) * 512], ps[:, :], 0.0,
                                              xr[:, mt * 512:(mt + 1) * 512], AL.add, AL.add),
                       waits=[(S_T, E.val(S_T)), ("s_out", xrv)])
                bank_free[id(ps)] = (S_V, E.val(S_V))
                E.emit("scalar", lambda s, mt=mt:
                       s.activation(out=rqb[:, mt * 512:(mt + 1) * 512],
                                    in_=rsb[:, mt * 512:(mt + 1) * 512], func=AF.Square),
                       waits=[(S_V, E.val(S_V))])
            sqv = E.val(S_S)
            rv = E.val(S_V)
            for mt in range(8):
                E.emit("tensor", lambda t, mt=mt:
                       t.matmul(psR[:, :], onecol[:, :], rsb.ap().bitcast(F32)[:, mt * 512:(mt + 1) * 512]
                                if False else rsb[:, mt * 512:(mt + 1) * 512],
                                start=(mt == 0), stop=(mt == 7)),
                       waits=([(S_V, rv), (S_V, onev)] if mt == 0 else []),
                       inc=(S_T if mt == 7 else None))
            for mt in range(8):
                E.emit("tensor", lambda t, mt=mt:
                       t.matmul(psQ[:, :], onecol[:, :], rqb[:, mt * 512:(mt + 1) * 512],
                                start=(mt == 0), stop=(mt == 7)),
                       waits=([(S_S, sqv)] if mt == 0 else []),
                       inc=(S_T if mt == 7 else None))
            stv = E.val(S_T)
            E.emit("vector", lambda v: v.tensor_scalar(mrow[:, :], psR[0:1, :], 1.0 / D, None, AL.mult),
                   waits=[(S_T, stv)])
            E.emit("scalar", lambda s: s.activation(out=vrow[:, :], in_=mrow[:, :], func=AF.Square),
                   waits=[(S_V, E.val(S_V))])
            E.emit("vector", lambda v: v.scalar_tensor_tensor(vrow[:, :], psQ[0:1, :], 1.0 / D,
                                                              vrow[:, :], AL.mult, AL.subtract),
                   waits=[(S_S, E.val(S_S))])
            E.emit("scalar", lambda s: s.activation(out=arow[:, :], in_=vrow[:, :], func=AF.Ln,
                                                    bias=epscol[0:1, 0:1]),
                   waits=[(S_V, E.val(S_V))])
            E.emit("scalar", lambda s: s.activation(out=arow[:, :], in_=arow[:, :], func=AF.Exp, scale=-0.5))
            E.emit("vector", lambda v: v.scalar_tensor_tensor(brow[:, :], mrow[:, :], -1.0,
                                                              arow[:, :], AL.mult, AL.mult),
                   waits=[(S_S, E.val(S_S))])
            rowv = E.val(S_V)
            E.emit("tensor", lambda t: t.matmul(psM[:, :], onerow[:, :], arow[:, :],
                                                start=True, stop=True),
                   waits=[(S_V, rowv)], inc=S_T)
            E.emit("tensor", lambda t: t.matmul(psN[:, :], onerow[:, :], brow[:, :],
                                                start=True, stop=True), inc=S_T)
            bcv = E.val(S_T)
            for mt in range(8):
                E.emit("vector", lambda v, mt=mt: v.tensor_tensor(
                    x1b[:, mt * 512:(mt + 1) * 512], rsb[:, mt * 512:(mt + 1) * 512],
                    psM[:, :], AL.mult), waits=[(S_T, bcv)] if mt == 0 else [])
                E.emit("vector", lambda v, mt=mt: v.tensor_tensor(
                    x1b[:, mt * 512:(mt + 1) * 512], x1b[:, mt * 512:(mt + 1) * 512],
                    psN[:, :], AL.add))
            x1v = E.val(S_V)
            E.emit("sync", lambda g, fo=fo: g.dma_start(
                out=x1o[:, fo:fo + 512].rearrange("(c p) f -> p c f", p=128),
                in_=x1b.ap().rearrange("p (c f) -> p c f", c=8)),
                inc="s_out", amt=16, waits=[(S_V, x1v)])
            bank_free[id(psO)] = (S_V, E.val(S_V)) if False else bank_free[id(psO)]
            psmn_guard = E.val(S_V)
            for tt in range(4):
                to = fo + tt * 128
                for kc in range(8):
                    E.emit("tensor", lambda t, kc=kc, tt=tt: t.matmul(
                        psL[:, :], x1b[:, kc * 512 + tt * 128: kc * 512 + tt * 128 + 128],
                        wrb[:, kc * 8:(kc + 1) * 8], start=(kc == 0), stop=(kc == 7)),
                        waits=([(S_V, max(x1v, psl_free))] if kc == 0 else []),
                        inc=(S_T if kc == 7 else None))
                lgv = E.val(S_T)
                E.emit("vector", lambda v: v.tensor_reduce(mx1[:, :], psL[:, :], AX.X, AL.max),
                       waits=[(S_T, lgv)])
                E.emit("vector", lambda v: v.scalar_tensor_tensor(prb[:, :], psL[:, :], 0.0,
                                                                  mx1[:, 0:1].broadcast_to([128, 8]),
                                                                  AL.add, AL.subtract))
                psl_free = E.val(S_V)
                E.emit("scalar", lambda s: s.activation(out=prb[:, :], in_=prb[:, :], func=AF.Exp),
                       waits=[(S_V, E.val(S_V))])
                E.emit("vector", lambda v: v.tensor_reduce(mx1[:, :], prb[:, :], AX.X, AL.add),
                       waits=[(S_S, E.val(S_S))])
                E.emit("vector", lambda v: v.reciprocal(mx1[:, :], mx1[:, :]))
                E.emit("vector", lambda v: v.tensor_scalar(prb[:, :], prb[:, :], mx1[:, 0:1], None, AL.mult))
                E.emit("vector", lambda v: v.tensor_reduce(mx1[:, :], prb[:, :], AX.X, AL.max))
                E.emit("vector", lambda v, tt=tt, ft=ft: v.tensor_scalar(
                    rstg["g0"][:, ft * 4 + tt: ft * 4 + tt + 1], mx1[:, :], 1.0, None, AL.mult))
                E.emit("vector", lambda v: v.tensor_copy(mx8[:, :], mx1[:, 0:1].broadcast_to([128, 8])))
                E.emit("vector", lambda v: v.max_index(ixu[:, :], mx8[:, :], prb[:, :]))
                E.emit("vector", lambda v: v.tensor_copy(ixf[:, :], ixu[:, 0:1]))
                E.emit("vector", lambda v, tt=tt, ft=ft: v.tensor_scalar(
                    rstg["idx0"][:, ft * 4 + tt: ft * 4 + tt + 1], ixf[:, :], 1.0, None, AL.mult))
                E.emit("vector", lambda v: v.tensor_scalar(msk[:, :], iob[:, :], ixf[:, 0:1], None,
                                                           AL.is_equal))
                E.emit("vector", lambda v: v.scalar_tensor_tensor(pm8[:, :], msk[:, :], -2.0,
                                                                  prb[:, :], AL.mult, AL.add))
                E.emit("vector", lambda v: v.tensor_reduce(mx1[:, :], pm8[:, :], AX.X, AL.max))
                E.emit("vector", lambda v, tt=tt, ft=ft: v.tensor_scalar(
                    rstg["g1"][:, ft * 4 + tt: ft * 4 + tt + 1], mx1[:, :], 1.0, None, AL.mult))
                E.emit("vector", lambda v: v.tensor_copy(mx8[:, :], mx1[:, 0:1].broadcast_to([128, 8])))
                E.emit("vector", lambda v: v.max_index(ixu[:, :], mx8[:, :], pm8[:, :]))
                E.emit("vector", lambda v: v.tensor_copy(ixf[:, :], ixu[:, 0:1]))
                E.emit("vector", lambda v, tt=tt, ft=ft: v.tensor_scalar(
                    rstg["idx1"][:, ft * 4 + tt: ft * 4 + tt + 1], ixf[:, :], 1.0, None, AL.mult))
            E.wait("tensor", S_V, E.val(S_V))
        fv = E.val(S_V)
        for k in ("idx0", "idx1", "g0", "g1"):
            E.emit("sync", lambda g, k=k: g.dma_start(out=rto[k], in_=rstg[k][:, :]),
                   inc="s_out", amt=16, waits=[(S_V, fv)])
        E.wait("gpsimd", "s_out", E.val("s_out"))
        E.finalize(block)
        E.close()
    return nc


def build_p4():
    nc = bass.Bass("TRN2", target_bir_lowering=False, debug=False)
    N2 = 2 * T
    CCH = 2048
    idxc = nc.dram_tensor("idxcat", [2, N2], F32, kind="ExternalInput").ap()
    x1tm = nc.dram_tensor("x1tm", [2 * T, D], F32, kind="ExternalInput").ap()
    tokidf = nc.dram_tensor("tokidf", [128, 512], F32, kind="ExternalInput").ap()
    gatef = nc.dram_tensor("gatef", [128, 512], F32, kind="ExternalInput").ap()
    evec = nc.dram_tensor("evec", [16, 1], F32, kind="ExternalInput").ap()
    base640 = nc.dram_tensor("base640", [16, 1], F32, kind="ExternalInput").ap()
    gsel = nc.dram_tensor("gsel", [16, 2], F32, kind="ExternalInput").ap()
    identb = nc.dram_tensor("identb", [128, 128], F32, kind="ExternalInput").ap()
    wts = {}
    for j, nm in enumerate(("a", "b")):
        wts[f"wi_{nm}"] = nc.dram_tensor(f"wi_{nm}", [128, 8 * 4096], F32, kind="ExternalInput").ap()
        wts[f"wo_{nm}"] = nc.dram_tensor(f"wo_{nm}", [128, 32 * 1024], F32, kind="ExternalInput").ap()
    blko = nc.dram_tensor("blk", [1280, D], F32, kind="ExternalOutput").ap()
    rowo = nc.dram_tensor("rowf", [2, N2], F32, kind="ExternalOutput").ap()
    rowt = nc.dram_tensor("rowt", [2, N2], F32).ap()
    ltab = nc.dram_tensor("ltab", [10240, 2], F32).ap()

    import contextlib
    with contextlib.ExitStack() as stack:
        sb = lambda name, shape, dt: stack.enter_context(nc.sbuf_tensor(name, shape, dt))
        mask = sb("mask", [16, CCH], F32)
        ones16 = sb("ones16", [16, CCH], F32)
        csum = sb("csum", [16, CCH], F32)
        ctrb = sb("ctrb", [16, CCH], F32)
        evb = sb("evb4", [16, 1], F32)
        bsb = sb("bsb", [16, 1], F32)
        gselb = sb("gselb", [16, 2], F32)
        tmp2 = sb("tmp2", [2, CCH], F32)
        slotu = sb("slotu", [128, 512], U32)
        slotf = sb("slotf", [128, 512], F32)
        tksb = sb("tksb", [128, 512], F32)
        gfsb = sb("gfsb", [128, 512], F32)
        prs = sb("prs", [128, 2 * 512], F32)
        zro = sb("zro", [128, 160], F32)
        idb = sb("idb", [128, 128], BF16)
        wiwo = sb("wiwo", [128, 8 * 4096], BF16)
        xg = sb("xg", [128, 5 * 1024], BF16)
        xin = sb("xin", [128, 8 * 640], BF16)
        hbf = sb("hbf", [128, 32 * 640], BF16)
        outf = sb("outf", [128, 8 * 640], F32)
        blkb = sb("blkb", [128, 1024], F32)
        gcol = sb("gcol", [128, 5], F32)
        tkcol = sb("tkcol", [128, 5], U32)
        tkcolf = sb("tkcolf", [128, 5], F32)
        psS = stack.enter_context(nc.psum_tensor("psS", [2, 512], F32))
        psT1 = stack.enter_context(nc.psum_tensor("psT1", [128, 128], BF16))
        psT2 = stack.enter_context(nc.psum_tensor("psT2", [128, 128], F32))
        psA4 = stack.enter_context(nc.psum_tensor("psA4", [128, 512], F32))
        psB4 = stack.enter_context(nc.psum_tensor("psB4", [128, 512], F32))
        psC4 = stack.enter_context(nc.psum_tensor("psC4", [128, 128], F32))
        psD4 = stack.enter_context(nc.psum_tensor("psD4", [128, 128], F32))
        block = stack.enter_context(nc.Block())
        E = Emitter(nc)
        for s in (S_T, S_V, S_S, "s_ini", "s_g", "s_sc", "s_wl", "s_out"):
            E.sem(s)

        E.emit("gpsimd", lambda g: g.dma_start(out=evb.ap(), in_=evec), inc="s_ini", amt=16)
        E.emit("gpsimd", lambda g: g.dma_start(out=bsb.ap(), in_=base640), inc="s_ini", amt=16)
        E.emit("gpsimd", lambda g: g.dma_start(out=gselb.ap(), in_=gsel), inc="s_ini", amt=16)
        E.emit("gpsimd", lambda g: g.dma_start(out=tksb.ap(), in_=tokidf), inc="s_ini", amt=16)
        E.emit("gpsimd", lambda g: g.dma_start(out=gfsb.ap(), in_=gatef), inc="s_ini", amt=16)
        E.emit("gpsimd", lambda g: g.dma_start(out=idb.ap(), in_=identb), inc="s_ini", amt=16)
        ini = E.val("s_ini")
        E.emit("vector", lambda v: v.memset(ones16[:, :], 1.0))
        E.emit("vector", lambda v: v.memset(zro[:, :], 0.0))
        zrov = E.val(S_V)
        E.emit("gpsimd", lambda g: g.dma_start(
            out=ltab.rearrange("(p c) two -> p (c two)", p=128), in_=zro[:, :]),
            inc="s_sc", amt=16, waits=[(S_V, zrov)])

        nch = N2 // CCH
        for ch in range(nch):
            co = ch * CCH
            for g2 in range(2):
                E.emit("gpsimd", lambda g, g2=g2, co=co: g.dma_start(
                    out=ctrb.ap()[g2 * 8:(g2 + 1) * 8, :],
                    in_=idxc[g2:g2 + 1, co:co + CCH].broadcast_to([8, CCH])),
                    inc="s_g", amt=16, waits=[(S_V, E.val(S_V))])
            gv = E.val("s_g")
            E.emit("vector", lambda v: v.tensor_scalar(mask[:, :], ctrb[:, :], evb[:, 0:1], None,
                                                       AL.is_equal),
                   waits=[("s_g", gv), ("s_ini", ini)])
            if ch == 0:
                E.emit("vector", lambda v: v.tensor_tensor_scan(
                    out=csum[:, :], data0=ones16[:, :], data1=mask[:, :], initial=0.0,
                    op0=AL.mult, op1=AL.add))
            else:
                E.emit("vector", lambda v: v.tensor_tensor_scan(
                    out=csum[:, :], data0=ones16[:, :], data1=mask[:, :],
                    initial=csum[:, CCH - 1:CCH], op0=AL.mult, op1=AL.add))
            E.emit("vector", lambda v: v.tensor_scalar(ctrb[:, :], csum[:, :], 640.0, None, AL.is_le))
            E.emit("vector", lambda v: v.tensor_tensor(ctrb[:, :], ctrb[:, :], mask[:, :], AL.mult))
            E.emit("vector", lambda v: v.tensor_scalar(mask[:, :], csum[:, :], bsb[:, 0:1], None, AL.add))
            E.emit("vector", lambda v: v.tensor_tensor(ctrb[:, :], ctrb[:, :], mask[:, :], AL.mult))
            ctv = E.val(S_V)
            for j in range(CCH // 512):
                E.emit("tensor", lambda t, j=j: t.matmul(
                    psS[:, :], gselb[:, :], ctrb[:, j * 512:(j + 1) * 512],
                    start=True, stop=True),
                    waits=([(S_V, ctv)] if j == 0 else [(S_V, E.val(S_V))]), inc=S_T)
                E.emit("vector", lambda v, j=j, co=co: v.tensor_copy(
                    tmp2[:, j * 512:(j + 1) * 512], psS[0:2, :]), waits=[(S_T, E.val(S_T))])
            E.emit("vector", lambda v: v.tensor_scalar(mask[0:2, 0:CCH], tmp2[:, :], 0.0, None,
                                                       AL.is_equal))
            E.emit("vector", lambda v: v.scalar_tensor_tensor(tmp2[:, :], mask[0:2, 0:CCH], 1e9,
                                                              tmp2[:, :], AL.mult, AL.add))
            E.emit("vector", lambda v: v.tensor_scalar(tmp2[:, :], tmp2[:, :], -1.0, None, AL.add))
            rbv = E.val(S_V)
            E.emit("sync", lambda g, co=co: g.dma_start(out=rowo[:, co:co + CCH], in_=tmp2[:, :]),
                   inc="s_out", amt=16, waits=[(S_V, rbv)])
            E.emit("sync", lambda g, co=co: g.dma_start(out=rowt[:, co:co + CCH], in_=tmp2[:, :]),
                   inc="s_out", amt=16)
            E.wait("vector", "s_out", E.val("s_out"))
        rv = E.val("s_out")
        E.emit("sync", lambda g: g.dma_start(
            out=slotf.ap(), in_=rowt.rearrange("g (p c) -> (g p) c", p=64)
            if False else rowt.rearrange("two (p c) -> two p c", p=64).rearrange("two p c -> (two p) c")),
            inc="s_out", amt=16, waits=[("s_out", rv)])
        E.emit("vector", lambda v: v.tensor_copy(slotu[:, :], slotf[:, :]),
               waits=[("s_out", E.val("s_out"))])
        E.emit("vector", lambda v: v.tensor_copy(
            prs.ap().rearrange("p (c two) -> p c two", two=2)[:, :, 0:1], tksb.ap().unsqueeze(2)))
        E.emit("vector", lambda v: v.tensor_copy(
            prs.ap().rearrange("p (c two) -> p c two", two=2)[:, :, 1:2], gfsb.ap().unsqueeze(2)))
        prv = E.val(S_V)
        E.wait("gpsimd", S_V, prv)
        E.wait("gpsimd", "s_sc", 16)
        regc = {}
        E.emit("gpsimd", lambda g: regc.__setitem__("b", g.to_reg(10239)))
        for i in range(512):
            E.emit("gpsimd", lambda g, i=i: g.indirect_dma_start(
                out=ltab, out_offset=bass.IndirectOffsetOnAxis(ap=slotu[:, i:i + 1], axis=0),
                in_=prs[:, 2 * i:2 * i + 2], in_offset=None,
                bounds_check=regc["b"], oob_is_err=False),
                inc="s_sc", amt=16)
        scv = E.val("s_sc")

        blkrows = {nm: nc.dram_tensor(f"blkrows_{nm}", [128, 8], U32, kind="ExternalInput").ap()
                   for nm in ("a", "b")}
        brsb = sb("brsb", [128, 2 * 8], U32)
        tgt = sb("tgt", [128, 2], F32)
        idf = sb("idf", [128, 128], F32)
        E.emit("gpsimd", lambda g: g.dma_start(out=idf.ap(), in_=identb), inc="s_ini", amt=16)
        for jn, nm in enumerate(("a", "b")):
            E.emit("gpsimd", lambda g, nm=nm, jn=jn: g.dma_start(
                out=brsb.ap()[:, jn * 8:jn * 8 + 8], in_=blkrows[nm]), inc="s_ini", amt=16)
        ini2 = E.val("s_ini")

        for jn, nm in enumerate(("a", "b")):
            E.emit("gpsimd", lambda g, nm=nm: g.dma_start(out=wiwo.ap(), in_=wts[f"wi_{nm}"]),
                   inc="s_wl", amt=16, waits=[(S_T, E.val(S_T))])
            wl1 = E.val("s_wl")
            for st in range(5):
                E.emit("gpsimd", lambda g, st=st, jn=jn: g.indirect_dma_start(
                    out=tgt[:, :], out_offset=None, in_=ltab,
                    in_offset=bass.IndirectOffsetOnAxis(ap=brsb[:, jn * 8 + st:jn * 8 + st + 1], axis=0)),
                    inc="s_g", amt=16, waits=[("s_sc", scv), ("s_ini", ini2),
                                              (S_V, E.val(S_V))])
                gv2 = E.val("s_g")
                E.emit("vector", lambda v, st=st: v.tensor_copy(tkcolf[:, st:st + 1], tgt[:, 0:1]),
                       waits=[("s_g", gv2)])
                E.emit("vector", lambda v, st=st: v.tensor_copy(gcol[:, st:st + 1], tgt[:, 1:2]))
                E.emit("vector", lambda v, st=st: v.tensor_copy(tkcol[:, st:st + 1], tkcolf[:, st:st + 1]))
                tkv = E.val(S_V)
                E.emit("gpsimd", lambda g, st=st: g.indirect_dma_start(
                    out=xg[:, st * 1024:(st + 1) * 1024], out_offset=None, in_=x1tm,
                    in_offset=bass.IndirectOffsetOnAxis(ap=tkcol[:, st:st + 1], axis=0)),
                    inc="s_g", amt=16, waits=[(S_V, tkv)])
            gxv = E.val("s_g")
            for st in range(5):
                for dc in range(8):
                    E.emit("tensor", lambda t, st=st, dc=dc: t.transpose(
                        psT1[:, :], xg[:, st * 1024 + dc * 128: st * 1024 + dc * 128 + 128], idb[:, :]),
                        waits=([("s_g", gxv), (S_V, E.val(S_V))] if (st == 0 and dc == 0) else []),
                        inc=S_T)
                    E.emit("vector", lambda v, st=st, dc=dc: v.tensor_copy(
                        xin[:, dc * 640 + st * 128: dc * 640 + st * 128 + 128], psT1[:, :]),
                        waits=[(S_T, E.val(S_T))])
                    E.wait("tensor", S_V, E.val(S_V))
            xinv = E.val(S_V)
            cgs = ((0, 512, psA4), (512, 128, psC4))
            mm1_bank = {id(psA4): 0, id(psC4): 0}
            for mt in range(32):
                for (c0, cw_, ps) in cgs:
                    for kc in range(8):
                        w = []
                        if kc == 0:
                            w = [(S_S, mm1_bank[id(ps)])]
                            if mt == 0:
                                w += [(S_V, xinv), ("s_wl", wl1)]
                        E.emit("tensor", lambda t, mt=mt, kc=kc, c0=c0, cw_=cw_, ps=ps:
                               t.matmul(ps[:, 0:cw_], wiwo[:, kc * 4096 + mt * 128: kc * 4096 + mt * 128 + 128],
                                        xin[:, kc * 640 + c0: kc * 640 + c0 + cw_],
                                        start=(kc == 0), stop=(kc == 7)),
                               waits=[x for x in w if x[1] > 0], inc=(S_T if kc == 7 else None))
                    E.emit("scalar", lambda s, mt=mt, c0=c0, cw_=cw_, ps=ps:
                           s.activation(out=hbf[:, mt * 640 + c0: mt * 640 + c0 + cw_],
                                        in_=ps[:, 0:cw_], func=AF.Relu),
                           waits=[(S_T, E.val(S_T))])
                    mm1_bank[id(ps)] = E.val(S_S)
            h_done_s = E.val(S_S)
            E.emit("gpsimd", lambda g, nm=nm: g.dma_start(out=wiwo.ap(), in_=wts[f"wo_{nm}"]),
                   inc="s_wl", amt=16, waits=[(S_T, E.val(S_T))])
            wl2 = E.val("s_wl")
            mm2_bank = {id(psB4): 0, id(psD4): 0}
            for m2 in range(8):
                for (c0, cw_, ps) in ((0, 512, psB4), (512, 128, psD4)):
                    for kc in range(32):
                        w = []
                        if kc == 0:
                            w = [(S_V, mm2_bank[id(ps)])]
                            if m2 == 0:
                                w += [(S_S, h_done_s), ("s_wl", wl2)]
                        E.emit("tensor", lambda t, m2=m2, kc=kc, c0=c0, cw_=cw_, ps=ps:
                               t.matmul(ps[:, 0:cw_], wiwo[:, kc * 1024 + m2 * 128: kc * 1024 + m2 * 128 + 128],
                                        hbf[:, kc * 640 + c0: kc * 640 + c0 + cw_],
                                        start=(kc == 0), stop=(kc == 31)),
                               waits=[x for x in w if x[1] > 0], inc=(S_T if kc == 31 else None))
                    E.emit("vector", lambda v, m2=m2, c0=c0, cw_=cw_, ps=ps:
                           v.tensor_copy(outf[:, m2 * 640 + c0: m2 * 640 + c0 + cw_], ps[:, 0:cw_]),
                           waits=[(S_T, E.val(S_T))])
                    mm2_bank[id(ps)] = E.val(S_V)
            outv = E.val(S_V)
            for st in range(5):
                for m2 in range(8):
                    E.emit("tensor", lambda t, st=st, m2=m2: t.transpose(
                        psT2[:, :], outf[:, m2 * 640 + st * 128: m2 * 640 + st * 128 + 128], idf[:, :]),
                        waits=([(S_V, outv)] if (st == 0 and m2 == 0) else [(S_V, E.val(S_V))]),
                        inc=S_T)
                    E.emit("vector", lambda v, st=st, m2=m2: v.tensor_scalar(
                        blkb[:, m2 * 128: m2 * 128 + 128], psT2[:, :],
                        gcol[:, st:st + 1], None, AL.mult),
                        waits=[(S_T, E.val(S_T))])
                bv = E.val(S_V)
                E.emit("sync", lambda g, jn=jn, st=st: g.dma_start(
                    out=blko[jn * 640 + st * 128: jn * 640 + (st + 1) * 128, :],
                    in_=blkb[:, :]),
                    inc="s_out", amt=16, waits=[(S_V, bv)])
                E.wait("vector", "s_out", E.val("s_out"))
        E.wait("gpsimd", "s_out", E.val("s_out"))
        E.finalize(block)
        E.close()
    return nc


def build_p5():
    nc = bass.Bass("TRN2", target_bir_lowering=False, debug=False)
    NTT = TSL // 128
    x1in = nc.dram_tensor("x1tm_sl", [TSL, D], F32, kind="ExternalInput").ap()
    ltab4 = nc.dram_tensor("ltab4", [10240, D], F32, kind="ExternalInput").ap()
    ridx = nc.dram_tensor("ridx", [128, 64], F32, kind="ExternalInput").ap()
    x2o = nc.dram_tensor("x2tm", [TSL, D], F32, kind="ExternalOutput").ap()

    import contextlib
    with contextlib.ExitStack() as stack:
        sb = lambda name, shape, dt: stack.enter_context(nc.sbuf_tensor(name, shape, dt))
        x1t = sb("x1t", [128, 2 * D], F32)
        gg = sb("gg", [128, 2 * D], F32)
        acc = sb("acc", [128, D], F32)
        rix = sb("rix", [128, 64], F32)
        riu = sb("riu", [128, 2], U32)
        mcol = sb("mcol", [128, 1], F32)
        vcol = sb("vcol", [128, 1], F32)
        epsc = sb("epsc", [128, 1], F32)
        block = stack.enter_context(nc.Block())
        E = Emitter(nc)
        for s in (S_T, S_V, S_S, "s_ini", "s_x0", "s_x1", "s_g", "s_out"):
            E.sem(s)
        E.emit("gpsimd", lambda g: g.dma_start(out=rix.ap(), in_=ridx), inc="s_ini", amt=16)
        ini = E.val("s_ini")
        E.emit("vector", lambda v: v.memset(epsc[:, :], 1e-5))
        regc = {}
        E.emit("gpsimd", lambda g: regc.__setitem__("b", g.to_reg(10239)))
        slot_cons = {}
        for tt in range(NTT):
            sl = tt % 2
            sx = f"s_x{sl}"
            lw = [slot_cons[sl]] if sl in slot_cons else []
            E.emit("sync", lambda g, tt=tt, sl=sl: g.dma_start(
                out=x1t.ap()[:, sl * D:(sl + 1) * D],
                in_=x1in[tt * 128:(tt + 1) * 128, :]), inc=sx, amt=16, waits=lw)
            xv = E.val(sx)
            E.emit("vector", lambda v, tt=tt: v.tensor_copy(riu[:, 0:1], rix[:, tt:tt + 1]),
                   waits=[("s_ini", ini)])
            E.emit("vector", lambda v, tt=tt: v.tensor_copy(riu[:, 1:2], rix[:, 32 + tt:32 + tt + 1]))
            E.emit("vector", lambda v: v.memset(gg[:, :], 0.0))
            gv = E.val(S_V)
            for k in range(2):
                E.emit("gpsimd", lambda g, k=k: g.indirect_dma_start(
                    out=gg[:, k * D:(k + 1) * D], out_offset=None, in_=ltab4,
                    in_offset=bass.IndirectOffsetOnAxis(ap=riu[:, k:k + 1], axis=0),
                    bounds_check=regc["b"], oob_is_err=False),
                    inc="s_g", amt=16, waits=[(S_V, gv)])
            ggv = E.val("s_g")
            E.emit("vector", lambda v, sl=sl: v.tensor_tensor(
                acc[:, :], x1t[:, sl * D:(sl + 1) * D], gg[:, 0:D], AL.add),
                waits=[("s_g", ggv), (sx, xv)])
            E.emit("vector", lambda v: v.tensor_tensor(acc[:, :], acc[:, :], gg[:, D:2 * D], AL.add))
            slot_cons[sl] = (S_V, E.val(S_V))
            E.emit("vector", lambda v: v.tensor_reduce(mcol[:, :], acc[:, :], AX.X, AL.add))
            E.emit("vector", lambda v: v.tensor_scalar(mcol[:, :], mcol[:, :], 1.0 / D, None, AL.mult))
            E.emit("vector", lambda v: v.tensor_scalar(acc[:, :], acc[:, :], mcol[:, 0:1], None,
                                                       AL.subtract))
            E.emit("scalar", lambda s: s.activation(out=gg[:, 0:D], in_=acc[:, :], func=AF.Square),
                   waits=[(S_V, E.val(S_V))])
            E.emit("vector", lambda v: v.tensor_reduce(vcol[:, :], gg[:, 0:D], AX.X, AL.add),
                   waits=[(S_S, E.val(S_S))])
            E.emit("vector", lambda v: v.tensor_scalar(vcol[:, :], vcol[:, :], 1.0 / D, None, AL.mult))
            E.emit("scalar", lambda s: s.activation(out=vcol[:, :], in_=vcol[:, :], func=AF.Ln,
                                                    bias=epsc[:, 0:1]),
                   waits=[(S_V, E.val(S_V))])
            E.emit("scalar", lambda s: s.activation(out=vcol[:, :], in_=vcol[:, :], func=AF.Exp,
                                                    scale=-0.5))
            E.emit("vector", lambda v: v.tensor_scalar(acc[:, :], acc[:, :], vcol[:, 0:1], None,
                                                       AL.mult), waits=[(S_S, E.val(S_S))])
            E.emit("sync", lambda g, tt=tt: g.dma_start(
                out=x2o[tt * 128:(tt + 1) * 128, :], in_=acc[:, :]),
                inc="s_out", amt=16, waits=[(S_V, E.val(S_V))])
            E.wait("vector", "s_out", E.val("s_out"))
        E.wait("gpsimd", "s_out", E.val("s_out"))
        E.finalize(block)
        E.close()
    return nc


def build_p6():
    nc = bass.Bass("TRN2", target_bir_lowering=False, debug=False)
    TP1 = TSL + 1
    x2h = nc.dram_tensor("x2h", [D, TP1], F32, kind="ExternalInput").ap()
    maas = nc.dram_tensor("maas", [128, 48], F32, kind="ExternalInput").ap()
    tm1s = nc.dram_tensor("tm1s", [128, 8 * 160], F32, kind="ExternalInput").ap()
    tm2s = nc.dram_tensor("tm2s", [32, 5 * 1024], F32, kind="ExternalInput").ap()
    td1s = nc.dram_tensor("td1s", [128, 8 * 64], F32, kind="ExternalInput").ap()
    td2 = nc.dram_tensor("td2", [64, 1024], F32, kind="ExternalInput").ap()
    tdecc = nc.dram_tensor("tdecc", [128, 8], F32, kind="ExternalInput").ap()
    tdecrow = nc.dram_tensor("tdecrow", [1, 1024], F32, kind="ExternalInput").ap()
    wpj = {v: nc.dram_tensor(f"w{v}s", [128, 8 * 1024], F32, kind="ExternalInput").ap()
           for v in "rkvg"}
    okm = {v: nc.dram_tensor(f"{v}_km", [D, TSL], F32, kind="ExternalOutput").ap()
           for v in ("r", "k", "w", "g")}
    otm = {v: nc.dram_tensor(f"{v}_tm", [TSL, D], F32, kind="ExternalOutput").ap()
           for v in ("k", "v", "w")}

    import contextlib
    with contextlib.ExitStack() as stack:
        sb = lambda name, shape, dt: stack.enter_context(nc.sbuf_tensor(name, shape, dt))
        vxb = sb("vxb", [128, 8 * TSL], BF16)
        x2c = sb("x2c", [128, 2 * TP1], F32)
        sxc = sb("sxc", [128, TSL], F32)
        mixv = [sb(f"mixv{i}", [32, TSL], BF16) for i in range(5)]
        wsb = sb("wsb", [128, 8 * 1024], BF16)
        tm1b = sb("tm1b", [128, 8 * 160], BF16)
        tm2b = sb("tm2b", [32, 5 * 1024], BF16)
        td1b = sb("td1b", [128, 8 * 64], BF16)
        td2b = sb("td2b", [64, 1024], BF16)
        tdtb = sb("tdtb", [64, TSL], BF16)
        maab = sb("maab", [128, 48], F32)
        tdcb = sb("tdcb", [128, 8], F32)
        onesr = sb("onesr", [1, 128], F32)
        tdrow = sb("tdrow", [1, 1024], F32)
        ev1 = sb("ev1", [128, 512], F32)
        ev2 = sb("ev2", [128, 512], F32)
        psMA = stack.enter_context(nc.psum_tensor("psMA", [64, 512], F32))
        psMB = stack.enter_context(nc.psum_tensor("psMB", [64, 512], F32))
        psMC = stack.enter_context(nc.psum_tensor("psMC", [32, 512], F32))
        psV1 = stack.enter_context(nc.psum_tensor("psV1", [128, 512], F32))
        psV2 = stack.enter_context(nc.psum_tensor("psV2", [128, 512], F32))
        psP1 = stack.enter_context(nc.psum_tensor("psP1", [128, 512], F32))
        psP2 = stack.enter_context(nc.psum_tensor("psP2", [128, 512], F32))
        psTD = stack.enter_context(nc.psum_tensor("psTD", [64, 512], F32))
        block = stack.enter_context(nc.Block())
        E = Emitter(nc)
        for s in (S_T, S_V, S_S, "s_ini", "s_x0", "s_x1", "s_out"):
            E.sem(s)

        for nm, dst in (("maas", maab), ("tdecc", tdcb), ("tdecrow", tdrow)):
            pass
        E.emit("gpsimd", lambda g: g.dma_start(out=maab.ap(), in_=maas), inc="s_ini", amt=16)
        E.emit("gpsimd", lambda g: g.dma_start(out=tdcb.ap(), in_=tdecc), inc="s_ini", amt=16)
        E.emit("gpsimd", lambda g: g.dma_start(out=tdrow.ap(), in_=tdecrow), inc="s_ini", amt=16)
        E.emit("gpsimd", lambda g: g.dma_start(out=tm1b.ap(), in_=tm1s), inc="s_ini", amt=16)
        E.emit("gpsimd", lambda g: g.dma_start(out=tm2b.ap(), in_=tm2s), inc="s_ini", amt=16)
        E.emit("gpsimd", lambda g: g.dma_start(out=td1b.ap(), in_=td1s), inc="s_ini", amt=16)
        E.emit("gpsimd", lambda g: g.dma_start(out=td2b.ap(), in_=td2), inc="s_ini", amt=16)
        ini = E.val("s_ini")
        E.emit("vector", lambda v: v.memset(onesr[:, :], 1.0))

        NFT = TSL // 512
        x2_cons = {}
        pbank = {}

        def bankw(ps):
            return pbank.get(id(ps), (S_V, 0))

        def load_x2(kc, slot, extra=()):
            sem = f"s_x{slot}"
            lw = list(extra) + ([x2_cons[slot]] if slot in x2_cons else [])
            E.emit("sync", lambda g, kc=kc, slot=slot: g.dma_start(
                out=x2c.ap()[:, slot * TP1:(slot + 1) * TP1],
                in_=x2h[kc * 128:(kc + 1) * 128, :]), inc=sem, amt=16, waits=lw)
            return sem, E.val(sem)

        def sx_of(kc, slot, xw):
            E.emit("vector", lambda v, slot=slot: v.tensor_tensor(
                sxc[:, :], x2c[:, slot * TP1: slot * TP1 + TSL],
                x2c[:, slot * TP1 + 1: slot * TP1 + 1 + TSL], AL.subtract),
                waits=[xw])
            x2_cons[slot] = (S_V, E.val(S_V))

        for kc in range(8):
            slot = kc % 2
            sem, xv = load_x2(kc, slot)
            sx_of(kc, slot, (sem, xv))
            E.emit("vector", lambda v, kc=kc, slot=slot: v.scalar_tensor_tensor(
                vxb[:, kc * TSL:(kc + 1) * TSL], sxc[:, :], maab[:, 0 * 8 + kc: 0 * 8 + kc + 1],
                x2c[:, slot * TP1 + 1: slot * TP1 + 1 + TSL], AL.mult, AL.add),
                waits=[("s_ini", ini)])
            x2_cons[slot] = (S_V, E.val(S_V))
        xm1v = E.val(S_V)
        mgrp = ((psMA, 0, 64), (psMB, 64, 64), (psMC, 128, 32))
        for ft in range(NFT):
            fo = ft * 512
            for (ps, m0, mw_) in mgrp:
                for kc in range(8):
                    w = []
                    if kc == 0:
                        w = [(S_V, xm1v), (S_S, E.val(S_S))]
                    E.emit("tensor", lambda t, ps=ps, m0=m0, mw_=mw_, kc=kc, fo=fo:
                           t.matmul(ps[:, :], tm1b[:, kc * 160 + m0: kc * 160 + m0 + mw_],
                                    vxb[:, kc * TSL + fo: kc * TSL + fo + 512],
                                    start=(kc == 0), stop=(kc == 7)),
                           waits=[x for x in w if x[1] > 0], inc=(S_T if kc == 7 else None))
                n32 = mw_ // 32
                for j in range(n32):
                    vi = (m0 // 32) + j
                    E.emit("scalar", lambda s, ps=ps, j=j, vi=vi, fo=fo:
                           s.activation(out=mixv[vi][:, fo:fo + 512], in_=ps[j * 32:(j + 1) * 32, :],
                                        func=AF.Tanh),
                           waits=[(S_T, E.val(S_T))])
        mixs = E.val(S_S)

        out_stage_guard = [0]

        def do_proj_km(v, silu=False, tdadd=False, first_waits=()):
            bank = pbank
            for mt in range(8):
                for ft in range(NFT):
                    ps = psP1 if (mt * NFT + ft) % 2 == 0 else psP2
                    for kc in range(8):
                        w = [bankw(ps)] if kc == 0 else []
                        if mt == 0 and ft == 0 and kc == 0:
                            w = w + list(first_waits)
                        E.emit("tensor", lambda t, ps=ps, mt=mt, ft=ft, kc=kc:
                               t.matmul(ps[:, :], wsb[:, kc * 1024 + mt * 128: kc * 1024 + mt * 128 + 128],
                                        vxb[:, kc * TSL + ft * 512: kc * TSL + ft * 512 + 512],
                                        start=(kc == 0), stop=(kc == 7)),
                               waits=[x for x in w if x[1] > 0], inc=(S_T if kc == 7 else None))
                    eb = ev1 if (mt * NFT + ft) % 2 == 0 else ev2
                    if silu:
                        E.emit("scalar", lambda s, ps=ps, eb=eb: s.activation(
                            out=eb[:, :], in_=ps[:, :], func=AF.Silu),
                            waits=[(S_T, E.val(S_T)), ("s_out", out_stage_guard[0])])
                        sv = (S_S, E.val(S_S))
                    elif tdadd:
                        E.emit("vector", lambda v2, ps=ps, eb=eb, mt=mt: v2.tensor_scalar(
                            eb[:, :], ps[:, :], tdcb[:, mt:mt + 1], None, AL.add),
                            waits=[(S_T, E.val(S_T)), ("s_out", out_stage_guard[0])])
                        sv = (S_V, E.val(S_V))
                    else:
                        E.emit("vector", lambda v2, ps=ps, eb=eb: v2.tensor_copy(eb[:, :], ps[:, :]),
                               waits=[(S_T, E.val(S_T)), ("s_out", out_stage_guard[0])])
                        sv = (S_V, E.val(S_V))
                    pbank[id(ps)] = sv
                    E.emit("sync", lambda g, v=v, mt=mt, ft=ft, eb=eb: g.dma_start(
                        out=okm[v][mt * 128:(mt + 1) * 128, ft * 512:(ft + 1) * 512], in_=eb[:, :]),
                        inc="s_out", amt=16, waits=[sv])
                    out_stage_guard[0] = E.val("s_out")

        def do_proj_tm(v, src_lhs, KP, rhs_sb, onesadd=False, first_waits=()):
            first = [True]
            bank = pbank
            nk = 1024 // KP if KP == 128 else 1
            for tt in range(TSL // 128):
                for ci, c0 in enumerate((0, 512)):
                    ps = psP1 if (tt * 2 + ci) % 2 == 0 else psP2
                    nacc = (8 if KP == 128 else 1) + (1 if onesadd else 0)
                    acc = 0
                    for kc in range(8 if KP == 128 else 1):
                        w = [bankw(ps)] if acc == 0 else []
                        if first[0] and acc == 0:
                            w = w + list(first_waits)
                            first[0] = False
                        st = (acc == 0)
                        acc += 1
                        sp = (acc == nacc)
                        E.emit("tensor", lambda t, ps=ps, tt=tt, c0=c0, kc=kc, st=st, sp=sp, v=v,
                               src_lhs=src_lhs, rhs_sb=rhs_sb, KP=KP:
                               t.matmul(ps[:, :], src_lhs[0:KP, kc * TSL + tt * 128: kc * TSL + tt * 128 + 128]
                                        if KP == 128 else src_lhs[0:KP, tt * 128: tt * 128 + 128],
                                        rhs_sb[0:KP, kc * 1024 + c0: kc * 1024 + c0 + 512]
                                        if KP == 128 else rhs_sb[0:KP, c0:c0 + 512],
                                        start=st, stop=sp),
                               waits=[x for x in w if x[1] > 0],
                               inc=(S_T if sp else None))
                    if onesadd:
                        st = False
                        E.emit("tensor", lambda t, ps=ps, tt=tt, c0=c0:
                               t.matmul(ps[:, :], onesr[0:1, 0:128], tdrow[0:1, c0:c0 + 512],
                                        start=False, stop=True), inc=S_T)
                    eb = ev1 if (tt * 2 + ci) % 2 == 0 else ev2
                    E.emit("vector", lambda v2, ps=ps, eb=eb: v2.tensor_copy(eb[:, :], ps[:, :]),
                           waits=[(S_T, E.val(S_T)), ("s_out", out_stage_guard[0])])
                    sv = (S_V, E.val(S_V))
                    pbank[id(ps)] = sv
                    E.emit("sync", lambda g, v=v, tt=tt, c0=c0, eb=eb: g.dma_start(
                        out=otm[v][tt * 128:(tt + 1) * 128, c0:c0 + 512], in_=eb[:, :]),
                        inc="s_out", amt=16, waits=[sv])
                    out_stage_guard[0] = E.val("s_out")

        for vi, vname in enumerate("wkvrg"):
            wname = {"k": "k", "v": "v", "r": "r", "g": "g"}.get(vname)
            if wname is not None:
                E.emit("gpsimd", lambda g, wname=wname: g.dma_start(out=wsb.ap(), in_=wpj[wname]),
                       inc="s_ini", amt=16, waits=[(S_T, E.val(S_T))])
            wv = E.val("s_ini")
            bankv = pbank
            for kc in range(8):
                slot = kc % 2
                sem, xv = load_x2(kc, slot)
                sx_of(kc, slot, (sem, xv))
                for ft in range(NFT):
                    fo = ft * 512
                    ps = psV1 if ft % 2 == 0 else psV2
                    w = [bankw(ps), (S_S, mixs)]
                    E.emit("tensor", lambda t, ps=ps, vi=vi, kc=kc, fo=fo:
                           t.matmul(ps[:, :], tm2b[0:32, vi * 1024 + kc * 128: vi * 1024 + kc * 128 + 128],
                                    mixv[vi][0:32, fo:fo + 512], start=True, stop=True),
                           waits=[x for x in w if x[1] > 0], inc=S_T)
                    E.emit("vector", lambda v2, ps=ps, vi=vi, kc=kc, fo=fo, slot=slot:
                           v2.scalar_tensor_tensor(ev1[:, :], ps[:, :],
                                                   maab[:, (vi + 1) * 8 + kc: (vi + 1) * 8 + kc + 1],
                                                   sxc[:, fo:fo + 512], AL.add, AL.mult),
                           waits=[(S_T, E.val(S_T)), ("s_out", out_stage_guard[0])])
                    E.emit("vector", lambda v2, kc=kc, fo=fo, slot=slot: v2.tensor_tensor(
                        vxb[:, kc * TSL + fo: kc * TSL + fo + 512], ev1[:, :],
                        x2c[:, slot * TP1 + 1 + fo: slot * TP1 + 1 + fo + 512], AL.add))
                    pbank[id(ps)] = (S_V, E.val(S_V))
                x2_cons[slot] = (S_V, E.val(S_V))
            E.wait("vector", "s_out", out_stage_guard[0])

            if vname == "w":
                bank = {id(psTD): (S_S, 0)}
                vxv = E.val(S_V)
                for ft in range(NFT):
                    for kc in range(8):
                        w = [bank[id(psTD)]] if kc == 0 else []
                        if ft == 0 and kc == 0:
                            w = w + [("s_ini", ini), (S_V, vxv)]
                        E.emit("tensor", lambda t, ft=ft, kc=kc:
                               t.matmul(psTD[:, :], td1b[:, kc * 64:(kc + 1) * 64],
                                        vxb[:, kc * TSL + ft * 512: kc * TSL + ft * 512 + 512],
                                        start=(kc == 0), stop=(kc == 7)),
                               waits=[x for x in w if x[1] > 0], inc=(S_T if kc == 7 else None))
                    E.emit("scalar", lambda s, ft=ft: s.activation(
                        out=tdtb[0:64, ft * 512:(ft + 1) * 512], in_=psTD[0:64, :], func=AF.Tanh),
                        waits=[(S_T, E.val(S_T))])
                    bank[id(psTD)] = (S_S, E.val(S_S))
                for mt in range(8):
                    for ft in range(NFT):
                        ps = psP1 if (mt * NFT + ft) % 2 == 0 else psP2
                        w = [bankw(ps), (S_S, E.val(S_S))]
                        E.emit("tensor", lambda t, ps=ps, mt=mt, ft=ft:
                               t.matmul(ps[:, :], td2b[0:64, mt * 128:(mt + 1) * 128],
                                        tdtb[0:64, ft * 512: ft * 512 + 512],
                                        start=True, stop=True),
                               waits=[x for x in w if x[1] > 0], inc=S_T)
                        eb = ev1 if (mt * NFT + ft) % 2 == 0 else ev2
                        E.emit("vector", lambda v2, ps=ps, eb=eb, mt=mt: v2.tensor_scalar(
                            eb[:, :], ps[:, :], tdcb[:, mt:mt + 1], None, AL.add),
                            waits=[(S_T, E.val(S_T)), ("s_out", out_stage_guard[0])])
                        sv = (S_V, E.val(S_V))
                        pbank[id(ps)] = sv
                        E.emit("sync", lambda g, mt=mt, ft=ft, eb=eb: g.dma_start(
                            out=okm["w"][mt * 128:(mt + 1) * 128, ft * 512:(ft + 1) * 512],
                            in_=eb[:, :]),
                            inc="s_out", amt=16, waits=[sv])
                        out_stage_guard[0] = E.val("s_out")
                do_proj_tm("w", tdtb, 64, td2b, onesadd=True)
            elif vname == "k":
                vxv = E.val(S_V)
                do_proj_km("k", first_waits=[("s_ini", wv), (S_V, vxv)])
                do_proj_tm("k", vxb, 128, wsb)
            elif vname == "v":
                vxv = E.val(S_V)
                do_proj_tm("v", vxb, 128, wsb, first_waits=[("s_ini", wv), (S_V, vxv)])
            elif vname == "r":
                vxv = E.val(S_V)
                do_proj_km("r", first_waits=[("s_ini", wv), (S_V, vxv)])
            elif vname == "g":
                vxv = E.val(S_V)
                do_proj_km("g", silu=True, first_waits=[("s_ini", wv), (S_V, vxv)])
        E.wait("gpsimd", "s_out", E.val("s_out"))
        E.finalize(block)
        E.close()
    return nc


def build_p7(nsc=None, stage=4, sub=9):
    nc = bass.Bass("TRN2", target_bir_lowering=False, debug=False)
    SC = 1024
    NSC = nsc or (T // SC)
    NCH = SC // 64
    rkm = [nc.dram_tensor(f"rkm_{i}", [64, T], F32, kind="ExternalInput").ap() for i in range(4)]
    kkm = [nc.dram_tensor(f"kkm_{i}", [64, T], F32, kind="ExternalInput").ap() for i in range(4)]
    wkm = [nc.dram_tensor(f"wkm_{i}", [64, T], F32, kind="ExternalInput").ap() for i in range(4)]
    ktm = [nc.dram_tensor(f"ktm_{i}", [T, 64], F32, kind="ExternalInput").ap() for i in range(4)]
    vtm = [nc.dram_tensor(f"vtm_{i}", [T, 64], F32, kind="ExternalInput").ap() for i in range(4)]
    wtm = [nc.dram_tensor(f"wtm_{i}", [T, 64], F32, kind="ExternalInput").ap() for i in range(4)]
    u4 = nc.dram_tensor("u4", [64, 4], F32, kind="ExternalInput").ap()
    lnwr = nc.dram_tensor("lnwrep", [128, 256], F32, kind="ExternalInput").ap()
    lnbr = nc.dram_tensor("lnbrep", [128, 256], F32, kind="ExternalInput").ap()
    umsk = nc.dram_tensor("umask", [64, 64], F32, kind="ExternalInput").ap()
    i64 = nc.dram_tensor("I64", [64, 64], F32, kind="ExternalInput").ap()
    trii = nc.dram_tensor("tri", [64, 64], F32, kind="ExternalInput").ap()
    eps128 = nc.dram_tensor("eps128", [128, 1], F32, kind="ExternalInput").ap()
    yo = [nc.dram_tensor(f"y_{i}", [T, 64], F32, kind="ExternalOutput").ap() for i in range(4)]

    import contextlib
    with contextlib.ExitStack() as stack:
        sb = lambda name, shape, dt: stack.enter_context(nc.sbuf_tensor(name, shape, dt))
        rk4 = [sb(f"rk4_{i}", [64, SC], F32) for i in range(4)]
        kk4 = [sb(f"kk4_{i}", [64, SC], F32) for i in range(4)]
        wk4 = [sb(f"wk4_{i}", [64, SC], F32) for i in range(4)]
        kt4 = [sb(f"kt4_{i}", [128, 8 * 64], F32) for i in range(4)]
        vt4 = [sb(f"vt4_{i}", [128, 8 * 64], BF16) for i in range(4)]
        wt4 = [sb(f"wt4_{i}", [128, 8 * 64], F32) for i in range(4)]
        eT4 = [sb(f"eT4_{i}", [128, 8 * 64], BF16) for i in range(4)]
        kT4 = [sb(f"kT4_{i}", [128, 8 * 64], BF16) for i in range(4)]
        ce4 = [sb(f"ce4_{i}", [64, SC + 1], F32) for i in range(4)]
        rt4 = [sb(f"rt4_{i}", [64, SC], BF16) for i in range(4)]
        kt4b = [sb(f"kt4b_{i}", [64, SC], BF16) for i in range(4)]
        rku4 = [sb(f"rku4_{i}", [64, SC], BF16) for i in range(4)]
        ybuf = [sb(f"ybuf_{i}", [128, 8 * 64], F32) for i in range(4)]
        ones64 = sb("ones64", [64, SC], F32)
        ends = sb("ends", [64, 4 * NCH], F32)
        endsh = sb("endsh", [64, 4 * NCH], F32)
        xnce = sb("xnce", [64, 4 * NCH], F32)
        diagt = sb("diagt", [64, 4 * NCH], F32)
        scr1 = sb("scr1", [64, SC], F32)
        scr2 = sb("scr2", [64, SC], F32)
        sq7 = sb("sq7", [128, 512], F32)
        smf = sb("smf", [128, 256], F32)
        smb = sb("smb", [128, 256], BF16)
        ddg = sb("ddg", [128, 256], BF16)
        sst = sb("sst", [64, 256], F32)
        ssb = sb("ssb", [64, 256], BF16)
        fct = sb("fct", [128, 256], BF16)
        u4b = sb("u4b", [64, 4], BF16)
        lnwb = sb("lnwb", [128, 256], F32)
        lnbb = sb("lnbb", [128, 256], F32)
        umb = sb("umb", [64, 64], F32)
        i64b = sb("i64b", [64, 64], F32)
        trib = sb("trib", [128, 64], BF16)
        epsb = sb("epsb", [128, 1], F32)
        mcol = sb("mcol7", [128, 8], F32)
        vcol = sb("vcol7", [128, 8], F32)
        psSM = stack.enter_context(nc.psum_tensor("psSM", [64, 256], F32))
        psY = stack.enter_context(nc.psum_tensor("psY", [64, 256], F32))
        psY2 = stack.enter_context(nc.psum_tensor("psY2", [64, 256], F32))
        psZ = stack.enter_context(nc.psum_tensor("psZ", [64, 256], F32))
        psDG = stack.enter_context(nc.psum_tensor("psDG", [64, 4], F32))
        psCB = stack.enter_context(nc.psum_tensor("psCB", [64, 256], F32))
        block = stack.enter_context(nc.Block())
        E = Emitter(nc)
        for s in (S_T, S_V, S_S, "s_ini", "s_ld", "s_ldg", "s_out"):
            E.sem(s)

        E.emit("gpsimd", lambda g: g.dma_start(out=u4b.ap(), in_=u4), inc="s_ini", amt=16)
        E.emit("gpsimd", lambda g: g.dma_start(out=lnwb.ap(), in_=lnwr), inc="s_ini", amt=16)
        E.emit("gpsimd", lambda g: g.dma_start(out=lnbb.ap(), in_=lnbr), inc="s_ini", amt=16)
        E.emit("gpsimd", lambda g: g.dma_start(out=umb.ap(), in_=umsk), inc="s_ini", amt=16)
        E.emit("gpsimd", lambda g: g.dma_start(out=i64b.ap(), in_=i64), inc="s_ini", amt=16)
        E.emit("gpsimd", lambda g: g.dma_start(out=trib.ap()[0:64, :], in_=trii), inc="s_ini", amt=16)
        E.emit("gpsimd", lambda g: g.dma_start(out=trib.ap()[64:128, :], in_=trii), inc="s_ini", amt=16)
        E.emit("gpsimd", lambda g: g.dma_start(out=epsb.ap(), in_=eps128), inc="s_ini", amt=16)
        ini = E.val("s_ini")
        E.emit("vector", lambda v: v.memset(sst[:, :], 0.0), waits=[("s_ini", ini)])
        E.emit("vector", lambda v: v.memset(ssb[:, :], 0.0))
        E.emit("vector", lambda v: v.memset(ones64[:, :], 1.0))
        for i in range(4):
            E.emit("vector", lambda v, i=i: v.memset(ce4[i][:, 0:1], 0.0))

        cb_free = [0]
        for sc in range(NSC):
            t0 = sc * SC
            for i in range(4):
                cons = (S_V, E.val(S_V))
                E.emit("sync", lambda g, i=i, t0=t0: g.dma_start(
                    out=rk4[i].ap(), in_=rkm[i][:, t0:t0 + SC]), inc="s_ld", amt=16, waits=[cons])
                E.emit("sync", lambda g, i=i, t0=t0: g.dma_start(
                    out=kk4[i].ap(), in_=kkm[i][:, t0:t0 + SC]), inc="s_ld", amt=16)
                E.emit("sync", lambda g, i=i, t0=t0: g.dma_start(
                    out=wk4[i].ap(), in_=wkm[i][:, t0:t0 + SC]), inc="s_ld", amt=16)
                E.emit("sync", lambda g, i=i, t0=t0: g.dma_start(
                    out=kt4[i].ap().rearrange("p (c v) -> p c v", c=8),
                    in_=ktm[i][t0:t0 + SC, :].rearrange("(c p) v -> p c v", p=128)),
                    inc="s_ld", amt=16)
                E.emit("sync", lambda g, i=i, t0=t0: g.dma_start(
                    out=wt4[i].ap().rearrange("p (c v) -> p c v", c=8),
                    in_=wtm[i][t0:t0 + SC, :].rearrange("(c p) v -> p c v", p=128)),
                    inc="s_ld", amt=16)
                ldv = E.val("s_ld")
                E.emit("gpsimd", lambda g, i=i, t0=t0: g.dma_start(
                    out=vt4[i].ap().rearrange("p (c v) -> p c v", c=8),
                    in_=vtm[i][t0:t0 + SC, :].rearrange("(c p) v -> p c v", p=128)),
                    inc="s_ldg", amt=16, waits=[(S_T, E.val(S_T))])
                E.emit("scalar", lambda s, i=i: s.activation(out=scr1[:, :], in_=wk4[i][:, :],
                                                             func=AF.Exp),
                       waits=[("s_ld", ldv)])
                E.emit("vector", lambda v, i=i: v.tensor_tensor_scan(
                    out=ce4[i][:, 1:SC + 1], data0=ones64[:, :], data1=scr1[:, :],
                    initial=ce4[i][:, 0:1], op0=AL.mult, op1=AL.add),
                    waits=[(S_S, E.val(S_S))])
                E.emit("vector", lambda v, i=i: v.tensor_copy(
                    ends[:, i * NCH:(i + 1) * NCH], ce4[i][:, 64::64]))
                E.emit("vector", lambda v, i=i: v.tensor_copy(endsh[:, i * NCH:i * NCH + 1],
                                                              ce4[i][:, 0:1]))
                E.emit("vector", lambda v, i=i: v.tensor_copy(
                    endsh[:, i * NCH + 1:(i + 1) * NCH], ends[:, i * NCH:(i + 1) * NCH - 1]))
                E.emit("vector", lambda v, i=i: v.tensor_tensor(
                    scr2.ap().rearrange("p (c q) -> p c q", c=NCH),
                    ce4[i].ap()[:, 1:SC + 1].rearrange("p (c q) -> p c q", c=NCH),
                    endsh.ap()[:, i * NCH:(i + 1) * NCH].unsqueeze(2).broadcast_to([64, NCH, 64]),
                    AL.subtract))
                E.emit("scalar", lambda s: s.activation(out=scr2[:, :], in_=scr2[:, :], func=AF.Exp),
                       waits=[(S_V, E.val(S_V))])
                E.emit("vector", lambda v, i=i: v.tensor_tensor(kt4b[i][:, :], kk4[i][:, :],
                                                                scr2[:, :], AL.mult),
                       waits=[(S_S, E.val(S_S))])
                E.emit("vector", lambda v, i=i: v.tensor_tensor(
                    scr2.ap().rearrange("p (c q) -> p c q", c=NCH),
                    ce4[i].ap()[:, 0:SC].rearrange("p (c q) -> p c q", c=NCH),
                    endsh.ap()[:, i * NCH:(i + 1) * NCH].unsqueeze(2).broadcast_to([64, NCH, 64]),
                    AL.subtract))
                E.emit("scalar", lambda s: s.activation(out=scr2[:, :], in_=scr2[:, :], func=AF.Exp,
                                                        scale=-1.0),
                       waits=[(S_V, E.val(S_V))])
                E.emit("vector", lambda v, i=i: v.tensor_tensor(rt4[i][:, :], rk4[i][:, :],
                                                                scr2[:, :], AL.mult),
                       waits=[(S_S, E.val(S_S))])
                E.emit("vector", lambda v, i=i: v.tensor_tensor(rku4[i][:, :], rk4[i][:, :],
                                                                kk4[i][:, :], AL.mult))
                E.emit("vector", lambda v, i=i: v.tensor_tensor(
                    xnce[:, i * NCH:(i + 1) * NCH], ends[:, i * NCH:(i + 1) * NCH],
                    endsh[:, i * NCH:(i + 1) * NCH], AL.subtract))
                E.emit("scalar", lambda s, i=i: s.activation(
                    out=xnce[:, i * NCH:(i + 1) * NCH], in_=xnce[:, i * NCH:(i + 1) * NCH],
                    func=AF.Exp, scale=-1.0), waits=[(S_V, E.val(S_V))])
                E.emit("vector", lambda v, i=i: v.tensor_copy(ce4[i][:, 0:1], ce4[i][:, SC:SC + 1]),
                       waits=[(S_S, E.val(S_S))])
                E.emit("scalar", lambda s, i=i: s.activation(out=eT4[i][:, :], in_=wt4[i][:, :],
                                                             func=AF.Exp),
                       waits=[("s_ld", ldv)])
            prep_s, prep_v = E.val(S_S), E.val(S_V)
            all_ld = E.val("s_ld")
            E.wait("tensor", "s_ldg", E.val("s_ldg"))
            if stage < 2:
                continue
            for c in range(NCH):
                base = (c % 2) * 64
                tile = c // 2
                for i in range(4):
                    w = []
                    if c == 0 and i == 0:
                        w = [(S_S, prep_s), ("s_ini", ini), ("s_ld", all_ld)]
                    if i == 0:
                        w.append((S_S, cb_free[0]))
                    E.emit("tensor", lambda t, i=i, base=base, tile=tile:
                           t.matmul(psCB[:, i * 64:(i + 1) * 64], trib[base:base + 64, :],
                                    eT4[i][base:base + 64, tile * 64:(tile + 1) * 64],
                                    start=True, stop=True),
                           waits=[x for x in w if x[1] > 0],
                           inc=(S_T if i == 3 else None))
                E.emit("scalar", lambda s, base=base: s.activation(
                    out=fct[base:base + 64, :], in_=psCB[:, :], func=AF.Exp),
                       waits=[(S_T, E.val(S_T)), (S_V, E.val(S_V))])
                cb_free[0] = E.val(S_S)
                fv = E.val(S_S)
                for i in range(4):
                    E.emit("vector", lambda v, i=i, base=base, tile=tile:
                           v.tensor_tensor(kT4[i][base:base + 64, tile * 64:(tile + 1) * 64],
                                           kt4[i][base:base + 64, tile * 64:(tile + 1) * 64],
                                           fct[base:base + 64, i * 64:(i + 1) * 64], AL.mult),
                           waits=[(S_S, fv)])
            kTv = E.val(S_V)
            if stage < 3:
                continue

            for c in range(NCH):
                base = (c % 2) * 64
                tile = c // 2
                for i in range(4):
                    w = [(S_V, kTv)] if (c == 0 and i == 0) else []
                    E.emit("tensor", lambda t, i=i, c=c:
                           t.matmul(psSM[:, i * 64:(i + 1) * 64], kt4b[i][:, c * 64:(c + 1) * 64],
                                    rt4[i][:, c * 64:(c + 1) * 64], start=True, stop=True),
                           waits=[x for x in w if x[1] > 0], inc=(S_T if i == 3 else None))
                if sub >= 1:
                    for i in range(4):
                        E.emit("tensor", lambda t, i=i, c=c:
                               t.matmul(psDG[:, i:i + 1], rku4[i][:, c * 64:(c + 1) * 64],
                                        u4b[:, i:i + 1], start=True, stop=True),
                               inc=(S_T if i == 3 else None))
                mmv = E.val(S_T)
                if sub < 2:
                    E.wait("vector", S_T, mmv)
                    E.emit("vector", lambda v: v.tensor_copy(diagt[:, 0:4], psDG[:, :]))
                    E.wait("tensor", S_V, E.val(S_V))
                    continue
                E.emit("vector", lambda v, base=base: v.tensor_tensor(
                    smf.ap()[base:base + 64, :].rearrange("p (u q) -> p u q", u=4),
                    psSM.ap().rearrange("p (u q) -> p u q", u=4),
                    umb.ap().unsqueeze(1).broadcast_to([64, 4, 64]), AL.mult),
                    waits=[(S_T, mmv)])
                E.emit("vector", lambda v, c=c: v.tensor_copy(diagt[:, 4 * c:4 * c + 4], psDG[:, :]))
                if sub < 3:
                    E.wait("tensor", S_V, E.val(S_V))
                    continue
                E.emit("vector", lambda v, c=c, base=base: v.tensor_tensor(
                    ddg.ap()[base:base + 64, :].rearrange("p (u q) -> p u q", u=4),
                    i64b.ap().unsqueeze(1).broadcast_to([64, 4, 64]),
                    diagt.ap()[:, 4 * c:4 * c + 4].unsqueeze(2).broadcast_to([64, 4, 64]),
                    AL.mult))
                E.emit("vector", lambda v, base=base: v.tensor_tensor(
                    smb[base:base + 64, :], smf[base:base + 64, :], ddg[base:base + 64, :], AL.add))
                smv = E.val(S_V)
                if sub < 4:
                    E.wait("tensor", S_V, E.val(S_V))
                    continue
                for i in range(4):
                    w = [(S_V, smv)] if i == 0 else []
                    E.emit("tensor", lambda t, i=i, base=base, tile=tile:
                           t.matmul(psY[:, i * 64:(i + 1) * 64],
                                    smb[base:base + 64, i * 64:(i + 1) * 64],
                                    vt4[i][base:base + 64, tile * 64:(tile + 1) * 64],
                                    start=True, stop=True),
                           waits=[x for x in w if x[1] > 0])
                for i in range(4):
                    E.emit("tensor", lambda t, i=i, c=c:
                           t.matmul(psY2[:, i * 64:(i + 1) * 64], rt4[i][:, c * 64:(c + 1) * 64],
                                    ssb[:, i * 64:(i + 1) * 64], start=True, stop=True),
                           inc=(S_T if i == 3 else None))
                if sub < 5:
                    E.wait("vector", S_T, E.val(S_T))
                    E.wait("tensor", S_V, E.val(S_V))
                    continue
                for i in range(4):
                    E.emit("tensor", lambda t, i=i, base=base, tile=tile:
                           t.matmul(psZ[:, i * 64:(i + 1) * 64],
                                    kT4[i][base:base + 64, tile * 64:(tile + 1) * 64],
                                    vt4[i][base:base + 64, tile * 64:(tile + 1) * 64],
                                    start=True, stop=True),
                           inc=(S_T if i == 3 else None))
                ymv = E.val(S_T)
                E.emit("vector", lambda v: v.tensor_tensor(sst[:, :], psZ[:, :], sst[:, :], AL.add),
                       waits=[(S_T, ymv)])
                E.emit("vector", lambda v, c=c: v.tensor_tensor(
                    sst.ap().rearrange("p (u q) -> p u q", u=4),
                    sst.ap().rearrange("p (u q) -> p u q", u=4),
                    xnce.ap().rearrange("p (u c) -> p u c", u=4)[:, :, c:c + 1]
                    .broadcast_to([64, 4, 64]), AL.mult))
                E.emit("vector", lambda v: v.tensor_copy(ssb[:, :], sst[:, :]))
                for i in range(4):
                    ew = [("s_out", E.val("s_out"))] if (c == 0 and i == 0) else []
                    E.emit("vector", lambda v, i=i, base=base, tile=tile:
                           v.tensor_scalar(ybuf[i][base:base + 64, tile * 64:(tile + 1) * 64],
                                           psY[:, i * 64:(i + 1) * 64], 0.125, None, AL.mult),
                           waits=ew)
                    E.emit("vector", lambda v, i=i, base=base, tile=tile:
                           v.scalar_tensor_tensor(ybuf[i][base:base + 64, tile * 64:(tile + 1) * 64],
                                                  psY2[:, i * 64:(i + 1) * 64], 0.125,
                                                  ybuf[i][base:base + 64, tile * 64:(tile + 1) * 64],
                                                  AL.mult, AL.add),
                           waits=ew)
                E.wait("tensor", S_V, E.val(S_V))
            if stage < 4:
                continue
            for i in range(4):
                yv = ybuf[i]
                E.emit("vector", lambda v, yv=yv: v.tensor_reduce(
                    mcol[:, :], yv.ap().rearrange("p (c q) -> p c q", c=8), AX.X, AL.add),
                    waits=[("s_out", E.val("s_out"))])
                E.emit("vector", lambda v: v.tensor_scalar(mcol[:, :], mcol[:, :], 1.0 / 64, None,
                                                           AL.mult))
                E.emit("vector", lambda v, yv=yv: v.tensor_tensor(
                    yv.ap().rearrange("p (c q) -> p c q", c=8),
                    yv.ap().rearrange("p (c q) -> p c q", c=8),
                    mcol.ap().unsqueeze(2).broadcast_to([128, 8, 64]), AL.subtract))
                E.emit("scalar", lambda s, yv=yv: s.activation(out=sq7[:, :], in_=yv[:, :],
                                                               func=AF.Square),
                       waits=[(S_V, E.val(S_V))])
                E.emit("vector", lambda v: v.tensor_reduce(
                    vcol[:, :], sq7.ap().rearrange("p (c q) -> p c q", c=8), AX.X, AL.add),
                    waits=[(S_S, E.val(S_S))])
                E.emit("vector", lambda v: v.tensor_scalar(vcol[:, :], vcol[:, :], 1.0 / 64, None,
                                                           AL.mult))
                E.emit("scalar", lambda s: s.activation(out=vcol[:, :], in_=vcol[:, :], func=AF.Ln,
                                                        bias=epsb[:, 0:1]),
                       waits=[(S_V, E.val(S_V))])
                E.emit("scalar", lambda s: s.activation(out=vcol[:, :], in_=vcol[:, :], func=AF.Exp,
                                                        scale=-0.5))
                E.emit("vector", lambda v, yv=yv: v.tensor_tensor(
                    yv.ap().rearrange("p (c q) -> p c q", c=8),
                    yv.ap().rearrange("p (c q) -> p c q", c=8),
                    vcol.ap().unsqueeze(2).broadcast_to([128, 8, 64]), AL.mult),
                    waits=[(S_S, E.val(S_S))])
                E.emit("vector", lambda v, yv=yv, i=i: v.tensor_tensor(
                    yv.ap().rearrange("p (c q) -> p c q", c=8),
                    yv.ap().rearrange("p (c q) -> p c q", c=8),
                    lnwb.ap()[:, i * 64:(i + 1) * 64].unsqueeze(1).broadcast_to([128, 8, 64]),
                    AL.mult))
                E.emit("vector", lambda v, yv=yv, i=i: v.tensor_tensor(
                    yv.ap().rearrange("p (c q) -> p c q", c=8),
                    yv.ap().rearrange("p (c q) -> p c q", c=8),
                    lnbb.ap()[:, i * 64:(i + 1) * 64].unsqueeze(1).broadcast_to([128, 8, 64]),
                    AL.add))
                E.emit("sync", lambda g, yv=yv, i=i, t0=t0: g.dma_start(
                    out=yo[i][t0:t0 + SC, :].rearrange("(c p) v -> p c v", p=128),
                    in_=yv.ap().rearrange("p (c v) -> p c v", c=8)),
                    inc="s_out", amt=16, waits=[(S_V, E.val(S_V))])
        E.wait("gpsimd", "s_out", E.val("s_out"))
        E.finalize(block)
        E.close()
    return nc

def build_p8():
    nc = bass.Bass("TRN2", target_bir_lowering=False, debug=False)
    NTT = TSL // 128
    yin = nc.dram_tensor("yln", [D, TSL], F32, kind="ExternalInput").ap()
    gin = nc.dram_tensor("gkm", [D, TSL], F32, kind="ExternalInput").ap()
    xin = nc.dram_tensor("x2r", [D, TSL], F32, kind="ExternalInput").ap()
    wout = nc.dram_tensor("woTs", [128, 8 * 1024], F32, kind="ExternalInput").ap()
    wrt = nc.dram_tensor("wss", [128, 8 * 8], F32, kind="ExternalInput").ap()
    bsr = nc.dram_tensor("bsrow", [1, 8], F32, kind="ExternalInput").ap()
    x3o = nc.dram_tensor("x3T", [D, TSL], F32, kind="ExternalOutput").ap()
    pro = nc.dram_tensor("probs", [128, NTT * 8], F32, kind="ExternalOutput").ap()

    import contextlib
    with contextlib.ExitStack() as stack:
        sb = lambda name, shape, dt: stack.enter_context(nc.sbuf_tensor(name, shape, dt))
        wob = sb("wob", [128, 8 * 1024], BF16)
        wrb = sb("wrb", [128, 8 * 8], F32)
        bsb8 = sb("bsb8", [1, 8], F32)
        onecol = sb("onecol", [128, 1], F32)
        onerow = sb("onerow", [1, 128], F32)
        ytl = sb("ytl", [128, 2 * 512], F32)
        ztl = sb("ztl", [128, 2 * 512], F32)
        ygbf = sb("ygbf", [128, 8 * 512], BF16)
        xr = sb("xr", [128, 8 * 512], F32)
        rsb = sb("rsb", [128, 8 * 512], F32)
        rqb = sb("rqb", [128, 8 * 512], F32)
        x1b = sb("x1b", [128, 8 * 512], F32)
        epscol = sb("epscol", [1, 1], F32)
        mrow = sb("mrow", [1, 512], F32)
        vrow = sb("vrow", [1, 512], F32)
        arow = sb("arow", [1, 512], F32)
        brow = sb("brow", [1, 512], F32)
        prb = sb("prb", [128, 8], F32)
        mx1 = sb("mx1", [128, 1], F32)
        prstg = sb("prstg", [128, NTT * 8], F32)
        psO = stack.enter_context(nc.psum_tensor("psO", [128, 512], F32))
        psP = stack.enter_context(nc.psum_tensor("psP", [128, 512], F32))
        psR = stack.enter_context(nc.psum_tensor("psR", [1, 512], F32))
        psQ = stack.enter_context(nc.psum_tensor("psQ", [1, 512], F32))
        psM = stack.enter_context(nc.psum_tensor("psM", [128, 512], F32))
        psN = stack.enter_context(nc.psum_tensor("psN", [128, 512], F32))
        psL = stack.enter_context(nc.psum_tensor("psL", [128, 8], F32))
        block = stack.enter_context(nc.Block())
        E = Emitter(nc)
        for s in (S_T, S_V, S_S, "s_ini", "s_y0", "s_y1", "s_out"):
            E.sem(s)

        E.emit("gpsimd", lambda g: g.dma_start(out=wob.ap(), in_=wout), inc="s_ini", amt=16)
        E.emit("gpsimd", lambda g: g.dma_start(out=wrb.ap(), in_=wrt), inc="s_ini", amt=16)
        E.emit("gpsimd", lambda g: g.dma_start(out=bsb8.ap(), in_=bsr), inc="s_ini", amt=16)
        ini = E.val("s_ini")
        E.emit("vector", lambda v: v.memset(onecol[:, :], 1.0))
        E.emit("vector", lambda v: v.memset(onerow[:, :], 1.0))
        E.emit("vector", lambda v: v.memset(epscol[:, :], 1e-5))
        onev = E.val(S_V)

        bank_free = {id(psO): (S_S, 0), id(psP): (S_S, 0)}
        psl_free = 0
        slot_cons = {}
        for ft in range(8):
            fo = ft * 512
            for kc in range(8):
                slot = kc % 2
                sem = f"s_y{slot}"
                lw = [slot_cons[slot]] if slot in slot_cons else []
                E.emit("sync", lambda g, kc=kc, fo=fo, slot=slot: g.dma_start(
                    out=ytl.ap()[:, slot * 512:(slot + 1) * 512],
                    in_=yin[kc * 128:(kc + 1) * 128, fo:fo + 512]), inc=sem, amt=16, waits=lw)
                E.emit("sync", lambda g, kc=kc, fo=fo, slot=slot: g.dma_start(
                    out=ztl.ap()[:, slot * 512:(slot + 1) * 512],
                    in_=gin[kc * 128:(kc + 1) * 128, fo:fo + 512]), inc=sem, amt=16)
                E.emit("vector", lambda v, kc=kc, slot=slot: v.tensor_tensor(
                    ygbf[:, kc * 512:(kc + 1) * 512], ytl[:, slot * 512:(slot + 1) * 512],
                    ztl[:, slot * 512:(slot + 1) * 512], AL.mult),
                    waits=[(sem, E.val(sem))])
                slot_cons[slot] = (S_V, E.val(S_V))
            ygv = E.val(S_V)
            E.emit("sync", lambda g, fo=fo: g.dma_start(
                out=xr.ap().rearrange("p (c f) -> p c f", c=8),
                in_=xin[:, fo:fo + 512].rearrange("(c p) f -> p c f", p=128)),
                inc="s_out", amt=16, waits=[("s_out", E.val("s_out"))])
            xrv = E.val("s_out")
            for mt in range(8):
                ps = psO if mt % 2 == 0 else psP
                for kc in range(8):
                    w = [bank_free[id(ps)], (S_V, ygv if kc == 0 else 0),
                         ("s_ini", ini if (ft == 0 and mt == 0 and kc == 0) else 0)]
                    E.emit("tensor", lambda t, ps=ps, kc=kc, mt=mt:
                           t.matmul(ps[:, :], wob[:, kc * 1024 + mt * 128: kc * 1024 + mt * 128 + 128],
                                    ygbf[:, kc * 512:(kc + 1) * 512],
                                    start=(kc == 0), stop=(kc == 7)),
                           waits=[x for x in w if x[1] > 0], inc=(S_T if kc == 7 else None))
                E.emit("vector", lambda v, ps=ps, mt=mt:
                       v.scalar_tensor_tensor(rsb[:, mt * 512:(mt + 1) * 512], ps[:, :], 0.0,
                                              xr[:, mt * 512:(mt + 1) * 512], AL.add, AL.add),
                       waits=[(S_T, E.val(S_T)), ("s_out", xrv)])
                bank_free[id(ps)] = (S_V, E.val(S_V))
                E.emit("scalar", lambda s, mt=mt:
                       s.activation(out=rqb[:, mt * 512:(mt + 1) * 512],
                                    in_=rsb[:, mt * 512:(mt + 1) * 512], func=AF.Square),
                       waits=[(S_V, E.val(S_V))])
            sqv = E.val(S_S)
            rv = E.val(S_V)
            for mt in range(8):
                E.emit("tensor", lambda t, mt=mt:
                       t.matmul(psR[:, :], onecol[:, :], rsb[:, mt * 512:(mt + 1) * 512],
                                start=(mt == 0), stop=(mt == 7)),
                       waits=([(S_V, rv), (S_V, onev)] if mt == 0 else []),
                       inc=(S_T if mt == 7 else None))
            for mt in range(8):
                E.emit("tensor", lambda t, mt=mt:
                       t.matmul(psQ[:, :], onecol[:, :], rqb[:, mt * 512:(mt + 1) * 512],
                                start=(mt == 0), stop=(mt == 7)),
                       waits=([(S_S, sqv)] if mt == 0 else []),
                       inc=(S_T if mt == 7 else None))
            stv = E.val(S_T)
            E.emit("vector", lambda v: v.tensor_scalar(mrow[:, :], psR[0:1, :], 1.0 / D, None, AL.mult),
                   waits=[(S_T, stv)])
            E.emit("scalar", lambda s: s.activation(out=vrow[:, :], in_=mrow[:, :], func=AF.Square),
                   waits=[(S_V, E.val(S_V))])
            E.emit("vector", lambda v: v.scalar_tensor_tensor(vrow[:, :], psQ[0:1, :], 1.0 / D,
                                                              vrow[:, :], AL.mult, AL.subtract),
                   waits=[(S_S, E.val(S_S))])
            E.emit("scalar", lambda s: s.activation(out=arow[:, :], in_=vrow[:, :], func=AF.Ln,
                                                    bias=epscol[0:1, 0:1]),
                   waits=[(S_V, E.val(S_V))])
            E.emit("scalar", lambda s: s.activation(out=arow[:, :], in_=arow[:, :], func=AF.Exp, scale=-0.5))
            E.emit("vector", lambda v: v.scalar_tensor_tensor(brow[:, :], mrow[:, :], -1.0,
                                                              arow[:, :], AL.mult, AL.mult),
                   waits=[(S_S, E.val(S_S))])
            rowv = E.val(S_V)
            E.emit("tensor", lambda t: t.matmul(psM[:, :], onerow[:, :], arow[:, :],
                                                start=True, stop=True),
                   waits=[(S_V, rowv)], inc=S_T)
            E.emit("tensor", lambda t: t.matmul(psN[:, :], onerow[:, :], brow[:, :],
                                                start=True, stop=True), inc=S_T)
            bcv = E.val(S_T)
            for mt in range(8):
                E.emit("vector", lambda v, mt=mt: v.tensor_tensor(
                    x1b[:, mt * 512:(mt + 1) * 512], rsb[:, mt * 512:(mt + 1) * 512],
                    psM[:, :], AL.mult), waits=[(S_T, bcv)] if mt == 0 else [])
                E.emit("vector", lambda v, mt=mt: v.tensor_tensor(
                    x1b[:, mt * 512:(mt + 1) * 512], x1b[:, mt * 512:(mt + 1) * 512],
                    psN[:, :], AL.add))
            x1v = E.val(S_V)
            E.emit("sync", lambda g, fo=fo: g.dma_start(
                out=x3o[:, fo:fo + 512].rearrange("(c p) f -> p c f", p=128),
                in_=x1b.ap().rearrange("p (c f) -> p c f", c=8)),
                inc="s_out", amt=16, waits=[(S_V, x1v)])
            for tt in range(4):
                gt = ft * 4 + tt
                for kc in range(8):
                    E.emit("tensor", lambda t, kc=kc, tt=tt: t.matmul(
                        psL[:, :], x1b[:, kc * 512 + tt * 128: kc * 512 + tt * 128 + 128],
                        wrb[:, kc * 8:(kc + 1) * 8], start=(kc == 0), stop=False),
                        waits=([(S_V, max(x1v, psl_free))] if kc == 0 else []))
                E.emit("tensor", lambda t, tt=tt: t.matmul(
                    psL[:, :], x1b[0:1, tt * 128: tt * 128 + 128] if False else onerow[0:1, 0:128],
                    bsb8[0:1, :], start=False, stop=True), inc=S_T)
                lgv = E.val(S_T)
                E.emit("vector", lambda v: v.tensor_reduce(mx1[:, :], psL[:, :], AX.X, AL.max),
                       waits=[(S_T, lgv)])
                E.emit("vector", lambda v: v.scalar_tensor_tensor(prb[:, :], psL[:, :], 0.0,
                                                                  mx1[:, 0:1].broadcast_to([128, 8]),
                                                                  AL.add, AL.subtract))
                psl_free = E.val(S_V)
                E.emit("scalar", lambda s: s.activation(out=prb[:, :], in_=prb[:, :], func=AF.Exp),
                       waits=[(S_V, E.val(S_V))])
                E.emit("vector", lambda v: v.tensor_reduce(mx1[:, :], prb[:, :], AX.X, AL.add),
                       waits=[(S_S, E.val(S_S))])
                E.emit("vector", lambda v: v.reciprocal(mx1[:, :], mx1[:, :]))
                E.emit("vector", lambda v, gt=gt: v.tensor_scalar(
                    prstg[:, gt * 8:(gt + 1) * 8], prb[:, :], mx1[:, 0:1], None, AL.mult))
            E.wait("tensor", S_V, E.val(S_V))
        fv = E.val(S_V)
        E.emit("sync", lambda g: g.dma_start(out=pro, in_=prstg[:, :]), inc="s_out", amt=16,
               waits=[(S_V, fv)])
        E.wait("gpsimd", "s_out", E.val("s_out"))
        E.finalize(block)
        E.close()
    return nc


def build_p9():
    nc = bass.Bass("TRN2", target_bir_lowering=False, debug=False)
    N2 = 2 * T
    KCAP = N2 // 8
    pexp = nc.dram_tensor("pexp", [128, 256], F32, kind="ExternalInput").ap()
    tok9 = nc.dram_tensor("tokid9", [128, 256], F32, kind="ExternalInput").ap()
    x3tm = nc.dram_tensor("x3tm", [N2, D], F32, kind="ExternalInput").ap()
    wi9 = nc.dram_tensor("wi9", [128, 8 * 4096], F32, kind="ExternalInput").ap()
    wo9 = nc.dram_tensor("wo9", [128, 32 * 1024], F32, kind="ExternalInput").ap()
    b1r = nc.dram_tensor("b1row", [1, 4096], F32, kind="ExternalInput").ap()
    b2r = nc.dram_tensor("b2row", [1, 1024], F32, kind="ExternalInput").ap()
    tril = nc.dram_tensor("trilow", [128, 128], F32, kind="ExternalInput").ap()
    bits31 = nc.dram_tensor("bits31", [128, 31], U32, kind="ExternalInput").ap()
    identb = nc.dram_tensor("identb", [128, 128], F32, kind="ExternalInput").ap()
    ebuf = nc.dram_tensor("ebuf", [N2, D], F32, kind="ExternalOutput").ap()
    emo = nc.dram_tensor("emask", [128, 256], F32, kind="ExternalOutput").ap()
    ttab = nc.dram_tensor("ttab9", [KCAP, 2], F32).ap()

    import contextlib
    with contextlib.ExitStack() as stack:
        sb = lambda name, shape, dt: stack.enter_context(nc.sbuf_tensor(name, shape, dt))
        psb = sb("psb", [128, 256], F32)
        thc = sb("thc", [128, 1], U32)
        cnd = sb("cnd", [128, 1], U32)
        cmpb = sb("cmpb", [128, 256], F32)
        cntc = sb("cntc", [128, 1], F32)
        totc = sb("totc", [1, 1], F32)
        msel = sb("msel", [128, 1], U32)
        selgt = sb("selgt", [128, 256], F32)
        eqm = sb("eqm", [128, 256], F32)
        sel = sb("sel", [128, 256], F32)
        csum9 = sb("csum9", [128, 256], F32)
        ones9 = sb("ones9", [128, 256], F32)
        onecol = sb("onecol9", [128, 1], F32)
        onerow = sb("onerow9", [1, 128], F32)
        o512 = sb("o512", [1, 512], F32)
        needc = sb("needc", [128, 1], F32)
        prec = sb("prec", [128, 1], F32)
        slotf = sb("slotf9", [128, 256], F32)
        slotu = sb("slotu9", [128, 256], U32)
        prs = sb("prs9", [128, 2 * 256], F32)
        tk9s = sb("tk9s", [128, 256], F32)
        trisb = sb("trisb", [128, 128], F32)
        idb = sb("idb9", [128, 128], BF16)
        idf = sb("idf9", [128, 128], F32)
        wiwo = sb("wiwo9", [128, 8 * 4096], BF16)
        b1s = sb("b1s", [1, 4096], F32)
        b2s = sb("b2s", [1, 1024], F32)
        xg = sb("xg9", [128, 1024], BF16)
        xin = sb("xin9", [128, 8 * 1024], BF16)
        hbf = sb("hbf9", [128, 32 * 1024], BF16)
        outf = sb("outf9", [128, 1024], F32)
        blk8 = sb("blk8", [128, 8 * 1024], BF16)
        g1 = sb("g1sc", [128, 512], F32)
        g2 = sb("g2sc", [128, 512], F32)
        tokf = sb("tokf", [128, 8], F32)
        toku = sb("toku", [128, 8], U32)
        pcol = sb("pcol9", [128, 8], F32)
        tgt = sb("tgt9", [128, 2], F32)
        bitsb = sb("bitsb", [128, 31], U32)
        psA = stack.enter_context(nc.psum_tensor("psA9", [128, 512], F32))
        psB = stack.enter_context(nc.psum_tensor("psB9", [128, 512], F32))
        psT1 = stack.enter_context(nc.psum_tensor("psT19", [128, 128], BF16))
        psT2 = stack.enter_context(nc.psum_tensor("psT29", [128, 128], F32))
        ps11 = stack.enter_context(nc.psum_tensor("ps119", [1, 1], F32))
        psBC = stack.enter_context(nc.psum_tensor("psBC9", [128, 1], F32))
        block = stack.enter_context(nc.Block())
        E = Emitter(nc)
        for s in (S_T, S_V, S_S, "s_ini", "s_sc", "s_g", "s_wl", "s_out", "s_eb"):
            E.sem(s)

        E.emit("gpsimd", lambda g: g.dma_start(out=psb.ap(), in_=pexp), inc="s_ini", amt=16)
        E.emit("gpsimd", lambda g: g.dma_start(out=tk9s.ap(), in_=tok9), inc="s_ini", amt=16)
        E.emit("gpsimd", lambda g: g.dma_start(out=trisb.ap(), in_=tril), inc="s_ini", amt=16)
        E.emit("gpsimd", lambda g: g.dma_start(out=idb.ap(), in_=identb), inc="s_ini", amt=16)
        E.emit("gpsimd", lambda g: g.dma_start(out=idf.ap(), in_=identb), inc="s_ini", amt=16)
        E.emit("gpsimd", lambda g: g.dma_start(out=b1s.ap(), in_=b1r), inc="s_ini", amt=16)
        E.emit("gpsimd", lambda g: g.dma_start(out=b2s.ap(), in_=b2r), inc="s_ini", amt=16)
        E.emit("gpsimd", lambda g: g.dma_start(out=bitsb.ap(), in_=bits31), inc="s_ini", amt=16)
        ini = E.val("s_ini")
        E.emit("vector", lambda v: v.memset(thc[:, :], 0), waits=[("s_ini", ini)])
        E.emit("vector", lambda v: v.memset(ones9[:, :], 1.0))
        E.emit("vector", lambda v: v.memset(onecol[:, :], 1.0))
        E.emit("vector", lambda v: v.memset(onerow[:, :], 1.0))
        E.emit("vector", lambda v: v.memset(o512[:, :], 1.0))
        pu = psb.ap().bitcast(U32)

        def count_ge(src_cnd, strict=False):
            op = AL.is_gt if strict else AL.is_ge
            E.emit("vector", lambda v, src_cnd=src_cnd, op=op: v.tensor_scalar(
                cmpb[:, :], psb[:, :], src_cnd.ap().bitcast(F32)[:, 0:1], 0.0, op,
                AL.add, accum_out=cntc[:, :]))
            E.emit("tensor", lambda t: t.matmul(ps11[:, :], onecol[:, :], cntc[:, :],
                                                start=True, stop=True),
                   waits=[(S_V, E.val(S_V)), (S_S, E.val(S_S))], inc=S_T)
            E.emit("vector", lambda v: v.tensor_copy(totc[:, :], ps11[0:1, :]),
                   waits=[(S_T, E.val(S_T))])
            E.emit("tensor", lambda t: t.matmul(psBC[:, :], onerow[:, :], totc[:, :],
                                                start=True, stop=True),
                   waits=[(S_V, E.val(S_V))], inc=S_T)

        for bit in range(30, -1, -1):
            j = 30 - bit
            E.emit("vector", lambda v, j=j: v.tensor_tensor(
                cnd[:, :], thc[:, :], bitsb[:, j:j + 1], AL.bitwise_or))
            count_ge(cnd)
            E.emit("vector", lambda v: v.tensor_scalar(msel[:, :], psBC[:, :], float(KCAP), None,
                                                       AL.is_ge), waits=[(S_T, E.val(S_T))])
            E.emit("vector", lambda v: v.select(thc[:, :], msel[:, :], cnd[:, :], thc[:, :], add_drain=True))

        count_ge(thc, strict=True)
        E.emit("vector", lambda v: v.tensor_copy(selgt[:, :], cmpb[:, :]),
               waits=[(S_T, E.val(S_T))])
        E.emit("vector", lambda v: v.tensor_scalar(needc[:, :], psBC[:, :], -1.0, float(KCAP),
                                                   AL.mult, AL.add))
        E.emit("vector", lambda v: v.tensor_scalar(eqm[:, :], psb[:, :],
                                                   thc.ap().bitcast(F32)[:, 0:1], None, AL.is_equal))

        def gcumsum(msk):
            E.emit("vector", lambda v, msk=msk: v.tensor_reduce(cntc[:, :], msk[:, :], AX.X, AL.add))
            E.emit("tensor", lambda t: t.matmul(psBC[:, :], trisb[:, :], cntc[:, :],
                                                start=True, stop=True),
                   waits=[(S_V, E.val(S_V))], inc=S_T)
            E.emit("vector", lambda v: v.tensor_copy(prec[:, :], psBC[:, :]),
                   waits=[(S_T, E.val(S_T))])
            E.emit("vector", lambda v, msk=msk: v.tensor_tensor_scan(
                out=csum9[:, :], data0=ones9[:, :], data1=msk[:, :], initial=0.0,
                op0=AL.mult, op1=AL.add))
            E.emit("vector", lambda v: v.tensor_scalar(csum9[:, :], csum9[:, :], prec[:, 0:1],
                                                       None, AL.add))

        gcumsum(eqm)
        E.emit("vector", lambda v: v.tensor_scalar(sel[:, :], csum9[:, :], needc[:, 0:1], None,
                                                   AL.is_le))
        E.emit("vector", lambda v: v.tensor_tensor(sel[:, :], sel[:, :], eqm[:, :], AL.mult))
        E.emit("vector", lambda v: v.tensor_tensor(sel[:, :], sel[:, :], selgt[:, :], AL.add))
        selv = E.val(S_V)
        E.emit("sync", lambda g: g.dma_start(out=emo, in_=sel[:, :]), inc="s_out", amt=16,
               waits=[(S_V, selv)])
        gcumsum(sel)
        E.emit("vector", lambda v: v.tensor_scalar(slotf[:, :], csum9[:, :], -1.0, None, AL.add))
        E.emit("vector", lambda v: v.tensor_scalar(cmpb[:, :], sel[:, :], -1e9, 1e9,
                                                   AL.mult, AL.add))
        E.emit("vector", lambda v: v.tensor_tensor(slotf[:, :], slotf[:, :], cmpb[:, :], AL.add))
        E.emit("vector", lambda v: v.tensor_copy(slotu[:, :], slotf[:, :]))
        E.emit("vector", lambda v: v.tensor_copy(
            prs.ap().rearrange("p (c two) -> p c two", two=2)[:, :, 0:1],
            tk9s.ap().unsqueeze(2)))
        E.emit("vector", lambda v: v.tensor_copy(
            prs.ap().rearrange("p (c two) -> p c two", two=2)[:, :, 1:2],
            psb.ap().unsqueeze(2)))
        prv = E.val(S_V)
        E.wait("gpsimd", S_V, prv)
        regc = {}
        E.emit("gpsimd", lambda g: regc.__setitem__("b", g.to_reg(KCAP - 1)))
        for i in range(256):
            E.emit("gpsimd", lambda g, i=i: g.indirect_dma_start(
                out=ttab, out_offset=bass.IndirectOffsetOnAxis(ap=slotu[:, i:i + 1], axis=0),
                in_=prs[:, 2 * i:2 * i + 2], in_offset=None,
                bounds_check=regc["b"], oob_is_err=False),
                inc="s_sc", amt=16)
        scv = E.val("s_sc")

        for sb4 in range(4):
            E.emit("gpsimd", lambda g: g.dma_start(out=wiwo.ap(), in_=wi9),
                   inc="s_wl", amt=16, waits=[(S_T, E.val(S_T))])
            wl1 = E.val("s_wl")
            for st in range(8):
                row0 = sb4 * 1024 + st * 128
                E.emit("gpsimd", lambda g, row0=row0: g.dma_start(
                    out=tgt[:, :], in_=ttab[row0:row0 + 128, :]),
                    inc="s_g", amt=16, waits=[("s_sc", scv), (S_V, E.val(S_V))])
                gv = E.val("s_g")
                E.emit("vector", lambda v, st=st: v.tensor_copy(tokf[:, st:st + 1], tgt[:, 0:1]),
                       waits=[("s_g", gv)])
                E.emit("vector", lambda v, st=st: v.tensor_copy(pcol[:, st:st + 1], tgt[:, 1:2]))
                E.emit("vector", lambda v, st=st: v.tensor_copy(toku[:, st:st + 1],
                                                                tokf[:, st:st + 1]))
                tkv = E.val(S_V)
                E.emit("gpsimd", lambda g, st=st: g.indirect_dma_start(
                    out=xg[:, :], out_offset=None, in_=x3tm,
                    in_offset=bass.IndirectOffsetOnAxis(ap=toku[:, st:st + 1], axis=0)),
                    inc="s_g", amt=16, waits=[(S_V, tkv), (S_T, E.val(S_T))])
                gxv = E.val("s_g")
                for dc in range(8):
                    E.emit("tensor", lambda t, st=st, dc=dc: t.transpose(
                        psT1[:, :], xg[:, dc * 128:(dc + 1) * 128], idb[:, :]),
                        waits=([("s_g", gxv), ("s_ini", ini)] if dc == 0 else []) +
                              [(S_V, E.val(S_V))],
                        inc=S_T)
                    E.emit("vector", lambda v, st=st, dc=dc: v.tensor_copy(
                        xin[:, dc * 1024 + st * 128: dc * 1024 + st * 128 + 128], psT1[:, :]),
                        waits=[(S_T, E.val(S_T))])
            xinv = E.val(S_V)
            bank = {id(psA): (S_S, 0), id(psB): (S_S, 0)}
            for mt in range(32):
                for ci, c0 in enumerate((0, 512)):
                    ps = psA if (mt * 2 + ci) % 2 == 0 else psB
                    for kc in range(8):
                        w = [bank[id(ps)], (S_V, E.val(S_V) if kc == 0 else 0)] if kc == 0 else []
                        if mt == 0 and ci == 0 and kc == 0:
                            w += [(S_V, xinv), ("s_wl", wl1)]
                        E.emit("tensor", lambda t, mt=mt, kc=kc, c0=c0, ps=ps:
                               t.matmul(ps[:, :], wiwo[:, kc * 4096 + mt * 128: kc * 4096 + mt * 128 + 128],
                                        xin[:, kc * 1024 + c0: kc * 1024 + c0 + 512],
                                        start=(kc == 0), stop=False),
                               waits=[x for x in w if x[1] > 0])
                    E.emit("tensor", lambda t, mt=mt, c0=c0, ps=ps:
                           t.matmul(ps[:, :], b1s[0:1, mt * 128:(mt + 1) * 128], o512[0:1, :],
                                    start=False, stop=True), inc=S_T)
                    mmv = E.val(S_T)
                    E.emit("scalar", lambda s, ps=ps: s.activation(out=g1[:, :], in_=ps[:, :],
                                                                   func=AF.Square),
                           waits=[(S_T, mmv)])
                    E.emit("vector", lambda v, ps=ps: v.tensor_tensor(g1[:, :], g1[:, :], ps[:, :],
                                                                      AL.mult),
                           waits=[(S_S, E.val(S_S))])
                    E.emit("vector", lambda v, ps=ps: v.scalar_tensor_tensor(
                        g1[:, :], g1[:, :], 0.044715, ps[:, :], AL.mult, AL.add))
                    E.emit("scalar", lambda s: s.activation(out=g1[:, :], in_=g1[:, :], func=AF.Tanh,
                                                            scale=0.7978845608028654),
                           waits=[(S_V, E.val(S_V))])
                    E.emit("vector", lambda v, ps=ps: v.tensor_scalar(g2[:, :], ps[:, :], 0.5, None,
                                                                      AL.mult),
                           waits=[(S_S, E.val(S_S))])
                    E.emit("vector", lambda v: v.tensor_scalar(g1[:, :], g1[:, :], 1.0, None, AL.add))
                    E.emit("vector", lambda v, mt=mt, c0=c0: v.tensor_tensor(
                        hbf[:, mt * 1024 + c0: mt * 1024 + c0 + 512], g1[:, :], g2[:, :], AL.mult))
                    bank[id(ps)] = (S_V, E.val(S_V))
            hv = E.val(S_V)
            E.emit("gpsimd", lambda g: g.dma_start(out=wiwo.ap()[:, 0:32 * 1024], in_=wo9),
                   inc="s_wl", amt=16, waits=[(S_T, E.val(S_T))])
            wl2 = E.val("s_wl")
            for m2 in range(8):
                for ci, c0 in enumerate((0, 512)):
                    ps = psA if ci == 0 else psB
                    for kc in range(32):
                        w = []
                        if kc == 0:
                            w = [bank[id(ps)]]
                            if m2 == 0 and ci == 0:
                                w += [(S_V, hv), ("s_wl", wl2)]
                        E.emit("tensor", lambda t, m2=m2, kc=kc, c0=c0, ps=ps:
                               t.matmul(ps[:, :], wiwo[:, kc * 1024 + m2 * 128: kc * 1024 + m2 * 128 + 128],
                                        hbf[:, kc * 1024 + c0: kc * 1024 + c0 + 512],
                                        start=(kc == 0), stop=False),
                               waits=[x for x in w if x[1] > 0])
                        if kc == 31:
                            E.emit("tensor", lambda t, m2=m2, c0=c0, ps=ps:
                                   t.matmul(ps[:, :], b2s[0:1, m2 * 128:(m2 + 1) * 128], o512[0:1, :],
                                            start=False, stop=True), inc=S_T)
                    E.emit("vector", lambda v, c0=c0, ps=ps: v.tensor_copy(outf[:, c0:c0 + 512],
                                                                           ps[:, :]),
                           waits=[(S_T, E.val(S_T))])
                    bank[id(ps)] = (S_V, E.val(S_V))
                ofv = E.val(S_V)
                for st in range(8):
                    E.emit("tensor", lambda t, st=st, m2=m2: t.transpose(
                        psT2[:, :], outf[:, st * 128:(st + 1) * 128], idf[:, :]),
                        waits=[(S_V, E.val(S_V))], inc=S_T)
                    E.emit("vector", lambda v, st=st, m2=m2: v.tensor_scalar(
                        blk8[:, st * 1024 + m2 * 128: st * 1024 + m2 * 128 + 128], psT2[:, :],
                        pcol[:, st:st + 1], None, AL.mult),
                        waits=[(S_T, E.val(S_T))])
            bv = E.val(S_V)
            for st in range(8):
                E.emit("gpsimd", lambda g, st=st: g.indirect_dma_start(
                    out=ebuf, out_offset=bass.IndirectOffsetOnAxis(ap=toku[:, st:st + 1], axis=0),
                    in_=blk8[:, st * 1024:(st + 1) * 1024], in_offset=None),
                    inc="s_eb", amt=16, waits=[(S_V, bv)])
            E.wait("vector", "s_eb", E.val("s_eb"))
            E.wait("tensor", "s_eb", E.val("s_eb"))
        E.wait("gpsimd", "s_out", E.val("s_out"))
        E.wait("gpsimd", "s_eb", E.val("s_eb"))
        E.finalize(block)
        E.close()
    return nc


def build_p10():
    nc = bass.Bass("TRN2", target_bir_lowering=False, debug=False)
    NTT = TSL // 128
    x3in = nc.dram_tensor("x3tm_sl", [TSL, D], F32, kind="ExternalInput").ap()
    ebs = [nc.dram_tensor(f"eb{e}", [TSL, D], F32, kind="ExternalInput").ap() for e in range(8)]
    mm8 = nc.dram_tensor("mm8", [128, 8 * 32], F32, kind="ExternalInput").ap()
    x4o = nc.dram_tensor("x4tm", [TSL, D], F32, kind="ExternalOutput").ap()

    import contextlib
    with contextlib.ExitStack() as stack:
        sb = lambda name, shape, dt: stack.enter_context(nc.sbuf_tensor(name, shape, dt))
        x3t = sb("x3t", [128, 2 * D], F32)
        ebt = sb("ebt", [128, 4 * D], F32)
        acc = sb("acc10", [128, D], F32)
        sq = sb("sq10", [128, D], F32)
        msb = sb("msb10", [128, 8 * 32], F32)
        mcol = sb("mcol10", [128, 1], F32)
        vcol = sb("vcol10", [128, 1], F32)
        epsc = sb("epsc10", [128, 1], F32)
        block = stack.enter_context(nc.Block())
        E = Emitter(nc)
        for s in (S_T, S_V, S_S, "s_ini", "s_x0", "s_x1", "s_e0", "s_e1", "s_e2", "s_e3", "s_out"):
            E.sem(s)
        E.emit("gpsimd", lambda g: g.dma_start(out=msb.ap(), in_=mm8), inc="s_ini", amt=16)
        ini = E.val("s_ini")
        E.emit("vector", lambda v: v.memset(epsc[:, :], 1e-5))
        slot_cons = {}
        for tt in range(NTT):
            sl = tt % 2
            sx = f"s_x{sl}"
            lw = [slot_cons[sl]] if sl in slot_cons else []
            E.emit("sync", lambda g, tt=tt, sl=sl: g.dma_start(
                out=x3t.ap()[:, sl * D:(sl + 1) * D],
                in_=x3in[tt * 128:(tt + 1) * 128, :]), inc=sx, amt=16, waits=lw)
            xv = E.val(sx)
            first = True
            for e in range(8):
                esl = e % 4
                se = f"s_e{esl}"
                ew = [slot_cons[("e", esl)]] if ("e", esl) in slot_cons else []
                E.emit("sync", lambda g, tt=tt, e=e, esl=esl: g.dma_start(
                    out=ebt.ap()[:, esl * D:(esl + 1) * D],
                    in_=ebs[e][tt * 128:(tt + 1) * 128, :]), inc=se, amt=16, waits=ew)
                ev = E.val(se)
                mc = msb[:, e * 32 + tt: e * 32 + tt + 1]
                if first:
                    E.emit("vector", lambda v, sl=sl, esl=esl, mc=mc: v.scalar_tensor_tensor(
                        acc[:, :], ebt[:, esl * D:(esl + 1) * D], mc,
                        x3t[:, sl * D:(sl + 1) * D], AL.mult, AL.add),
                        waits=[(se, ev), (sx, xv), ("s_ini", ini)])
                    first = False
                else:
                    E.emit("vector", lambda v, esl=esl, mc=mc: v.scalar_tensor_tensor(
                        acc[:, :], ebt[:, esl * D:(esl + 1) * D], mc,
                        acc[:, :], AL.mult, AL.add),
                        waits=[(se, ev)])
                slot_cons[("e", esl)] = (S_V, E.val(S_V))
            slot_cons[sl] = (S_V, E.val(S_V))
            E.emit("vector", lambda v: v.tensor_reduce(mcol[:, :], acc[:, :], AX.X, AL.add))
            E.emit("vector", lambda v: v.tensor_scalar(mcol[:, :], mcol[:, :], 1.0 / D, None, AL.mult))
            E.emit("vector", lambda v: v.tensor_scalar(acc[:, :], acc[:, :], mcol[:, 0:1], None,
                                                       AL.subtract))
            E.emit("scalar", lambda s: s.activation(out=sq[:, :], in_=acc[:, :], func=AF.Square),
                   waits=[(S_V, E.val(S_V))])
            E.emit("vector", lambda v: v.tensor_reduce(vcol[:, :], sq[:, :], AX.X, AL.add),
                   waits=[(S_S, E.val(S_S))])
            E.emit("vector", lambda v: v.tensor_scalar(vcol[:, :], vcol[:, :], 1.0 / D, None, AL.mult))
            E.emit("scalar", lambda s: s.activation(out=vcol[:, :], in_=vcol[:, :], func=AF.Ln,
                                                    bias=epsc[:, 0:1]),
                   waits=[(S_V, E.val(S_V))])
            E.emit("scalar", lambda s: s.activation(out=vcol[:, :], in_=vcol[:, :], func=AF.Exp,
                                                    scale=-0.5))
            E.emit("vector", lambda v: v.tensor_scalar(acc[:, :], acc[:, :], vcol[:, 0:1], None,
                                                       AL.mult), waits=[(S_S, E.val(S_S))])
            E.emit("sync", lambda g, tt=tt: g.dma_start(
                out=x4o[tt * 128:(tt + 1) * 128, :], in_=acc[:, :]),
                inc="s_out", amt=16, waits=[(S_V, E.val(S_V))])
            E.wait("vector", "s_out", E.val("s_out"))
        E.wait("gpsimd", "s_out", E.val("s_out"))
        E.finalize(block)
        E.close()
    return nc


def _swz2(a, kc):
    a = np.asarray(a, np.float32)
    return np.ascontiguousarray(a.reshape(kc, 128, -1).transpose(1, 0, 2).reshape(128, -1))


_CACHE = {}


def _get(buildfn):
    if buildfn.__name__ not in _CACHE:
        _CACHE[buildfn.__name__] = buildfn()
    return _CACHE[buildfn.__name__]


def kernel(**inputs):
    f32 = np.float32
    x0 = np.asarray(inputs["L0_attn_hidden_states"], f32)
    xT_full = np.ascontiguousarray(x0.transpose(2, 0, 1))

    nc1 = _get(build_p1)
    ims = [_p1_inputs_for_core(c, inputs, xT_full) for c in range(NCORES)]
    r1 = run_bass_kernel_spmd(nc1, ims, core_ids=list(range(NCORES)))
    p1o = {}
    for name, rows in (("u_f", DI), ("u_b", DI), ("delta_f", DI), ("delta_b", DI),
                       ("B_f", NST), ("B_b", NST), ("C_f", NST), ("C_b", NST)):
        full = np.empty((rows, B, T), f32)
        for c in range(NCORES):
            b, sl = divmod(c, 4)
            full[:, b, sl * TSL:(sl + 1) * TSL] = np.asarray(r1.results[c][name]).reshape(rows, TSL)
        p1o[name] = full
    zsil_c = [np.asarray(r1.results[c]["zsil"]).reshape(DI, TSL) for c in range(NCORES)]

    y2 = run_p2(p1o, inputs)

    nc3 = _get(build_p3)
    woutTs = _swz2(np.asarray(inputs["L0_attn_out_proj_w"], f32).T, 16)
    w_routs = _swz2(np.asarray(inputs["L0_ffn_w_router"], f32), 8)
    iota8 = np.tile(np.arange(8, dtype=f32), (128, 1))
    ims3 = []
    for c in range(NCORES):
        b, sl = divmod(c, 4)
        ts = slice(sl * TSL, (sl + 1) * TSL)
        ims3.append({
            "y": np.ascontiguousarray(y2[:, b, ts]),
            "zsil": zsil_c[c],
            "xTr": np.ascontiguousarray(xT_full[:, b, ts]),
            "woutTs": woutTs, "w_routs": w_routs, "iota8": iota8,
        })
    r3 = run_bass_kernel_spmd(nc3, ims3, core_ids=list(range(NCORES)))
    x1_fm = np.empty((D, B, T), f32)
    idxg = {k: np.empty((B, T), f32) for k in ("idx0", "idx1", "g0", "g1")}
    for c in range(NCORES):
        b, sl = divmod(c, 4)
        ts = slice(sl * TSL, (sl + 1) * TSL)
        x1_fm[:, b, ts] = np.asarray(r3.results[c]["x1T"]).reshape(D, TSL)
        for k in idxg:
            arr = np.asarray(r3.results[c][k]).reshape(128, TSL // 128)
            idxg[k][b, ts] = arr.T.reshape(-1)

    nc4 = _get(build_p4)
    x1tm = np.ascontiguousarray(x1_fm.transpose(1, 2, 0).reshape(B * T, D))
    idxcat = np.concatenate([idxg["idx0"], idxg["idx1"]], axis=1).astype(f32)
    gatecat = np.concatenate([idxg["g0"], idxg["g1"]], axis=1).astype(f32)
    jj = np.arange(65536, dtype=np.int64)
    tokidf = ((jj // 32768) * T + (jj % T)).reshape(128, 512).astype(f32)
    gatef = np.ascontiguousarray(gatecat.reshape(-1).reshape(128, 512))
    evec = (np.arange(16, dtype=f32) % 8).reshape(16, 1)
    base640 = ((np.arange(16) // 8 * 8 + np.arange(16) % 8) * 640.0).astype(f32).reshape(16, 1)
    gsel = np.zeros((16, 2), f32)
    gsel[0:8, 0] = 1.0
    gsel[8:16, 1] = 1.0
    ident = np.eye(128, dtype=f32)
    wi = np.asarray(inputs["L0_ffn_wi"], f32)
    wo = np.asarray(inputs["L0_ffn_wo"], f32)
    ims4 = []
    for c in range(NCORES):
        im = {"idxcat": idxcat, "x1tm": x1tm, "tokidf": tokidf, "gatef": gatef,
              "evec": evec, "base640": base640, "gsel": gsel, "identb": ident}
        for jn, nm in enumerate(("a", "b")):
            bk = 2 * c + jn
            e = bk % 8
            im[f"wi_{nm}"] = _swz2(wi[e], 8)
            im[f"wo_{nm}"] = _swz2(wo[e], 32)
            rows = (bk * 640 + np.arange(640, dtype=np.uint32)).reshape(5, 128).T
            br = np.zeros((128, 8), np.uint32)
            br[:, 0:5] = rows
            im[f"blkrows_{nm}"] = np.ascontiguousarray(br)
        ims4.append(im)
    r4 = run_bass_kernel_spmd(nc4, ims4, core_ids=list(range(NCORES)))
    ltab4 = np.concatenate([np.asarray(r4.results[c]["blk"]).reshape(1280, D)
                            for c in range(NCORES)], axis=0)
    rowf = np.asarray(r4.results[0]["rowf"]).reshape(2, 2 * T)

    nc5 = _get(build_p5)
    ims5 = []
    for c in range(NCORES):
        b, sl = divmod(c, 4)
        ts = slice(sl * TSL, (sl + 1) * TSL)
        ridx = np.zeros((128, 64), f32)
        ridx[:, 0:32] = rowf[b, 0 * T:1 * T][ts].reshape(32, 128).T
        ridx[:, 32:64] = rowf[b, 1 * T:2 * T][ts].reshape(32, 128).T
        ims5.append({"x1tm_sl": np.ascontiguousarray(x1tm[b * T + sl * TSL: b * T + (sl + 1) * TSL]),
                     "ltab4": ltab4, "ridx": ridx})
    r5 = run_bass_kernel_spmd(nc5, ims5, core_ids=list(range(NCORES)))
    x2 = np.empty((B, T, D), f32)
    for c in range(NCORES):
        b, sl = divmod(c, 4)
        x2[b, sl * TSL:(sl + 1) * TSL] = np.asarray(r5.results[c]["x2tm"]).reshape(TSL, D)

    nc6 = _get(build_p6)
    maa_names = ["x_maa", "w_maa", "k_maa", "v_maa", "r_maa", "g_maa"]
    maas = np.zeros((128, 48), f32)
    for vi, nm in enumerate(maa_names):
        maas[:, vi * 8:(vi + 1) * 8] = np.asarray(inputs[f"L1_attn_{nm}"], f32).reshape(8, 128).T
    p6shared = {
        "maas": maas,
        "tm1s": _swz2(inputs["L1_attn_tm_w1"], 8),
        "tm2s": np.ascontiguousarray(np.asarray(inputs["L1_attn_tm_w2"], f32)
                                     .transpose(1, 0, 2).reshape(32, 5 * 1024)),
        "td1s": _swz2(inputs["L1_attn_td_w1"], 8),
        "td2": np.asarray(inputs["L1_attn_td_w2"], f32),
        "tdecc": np.asarray(inputs["L1_attn_time_decay"], f32).reshape(-1).reshape(8, 128).T.copy(),
        "tdecrow": np.asarray(inputs["L1_attn_time_decay"], f32).reshape(1, 1024),
        "wrs": _swz2(inputs["L1_attn_Wr"], 8),
        "wks": _swz2(inputs["L1_attn_Wk"], 8),
        "wvs": _swz2(inputs["L1_attn_Wv"], 8),
        "wgs": _swz2(inputs["L1_attn_Wg"], 8),
    }
    ims6 = []
    for c in range(NCORES):
        b, sl = divmod(c, 4)
        t0 = sl * TSL
        x2h = np.zeros((D, TSL + 1), f32)
        x2h[:, 1:] = x2[b, t0:t0 + TSL].T
        if t0 > 0:
            x2h[:, 0] = x2[b, t0 - 1]
        im = dict(p6shared)
        im["x2h"] = x2h
        ims6.append(im)
    r6 = run_bass_kernel_spmd(nc6, ims6, core_ids=list(range(NCORES)))
    km = {v: np.empty((D, B, T), f32) for v in ("r", "k", "w", "g")}
    tm = {v: np.empty((B, T, D), f32) for v in ("k", "v", "w")}
    for c in range(NCORES):
        b, sl = divmod(c, 4)
        ts = slice(sl * TSL, (sl + 1) * TSL)
        for v in km:
            km[v][:, b, ts] = np.asarray(r6.results[c][f"{v}_km"]).reshape(D, TSL)
        for v in tm:
            tm[v][b, ts] = np.asarray(r6.results[c][f"{v}_tm"]).reshape(TSL, D)

    nc7 = _get(build_p7)
    tf = np.asarray(inputs["L1_attn_time_first"], f32)
    lnw = np.asarray(inputs["L1_attn_ln_w"], f32)
    lnb = np.asarray(inputs["L1_attn_ln_b"], f32)
    jj2, tt2 = np.meshgrid(np.arange(64), np.arange(64), indexing="ij")
    ims7 = []
    for c in range(NCORES):
        im = {
            "umask": (jj2 < tt2).astype(f32), "I64": np.eye(64, dtype=f32),
            "tri": (jj2 <= tt2).astype(f32), "eps128": np.full((128, 1), 1e-5, f32),
        }
        u4 = np.zeros((64, 4), f32)
        lnwrep = np.zeros((128, 256), f32)
        lnbrep = np.zeros((128, 256), f32)
        for i in range(4):
            uid = c * 4 + i
            b, h = uid // 16, uid % 16
            im[f"rkm_{i}"] = np.ascontiguousarray(km["r"][h * 64:(h + 1) * 64, b, :])
            im[f"kkm_{i}"] = np.ascontiguousarray(km["k"][h * 64:(h + 1) * 64, b, :])
            im[f"wkm_{i}"] = np.ascontiguousarray(km["w"][h * 64:(h + 1) * 64, b, :])
            im[f"ktm_{i}"] = np.ascontiguousarray(tm["k"][b, :, h * 64:(h + 1) * 64])
            im[f"vtm_{i}"] = np.ascontiguousarray(tm["v"][b, :, h * 64:(h + 1) * 64])
            im[f"wtm_{i}"] = np.ascontiguousarray(tm["w"][b, :, h * 64:(h + 1) * 64])
            u4[:, i] = tf[h]
            lnwrep[:, i * 64:(i + 1) * 64] = lnw[h * 64:(h + 1) * 64]
            lnbrep[:, i * 64:(i + 1) * 64] = lnb[h * 64:(h + 1) * 64]
        im["u4"] = u4
        im["lnwrep"] = lnwrep
        im["lnbrep"] = lnbrep
        ims7.append(im)
    r7 = run_bass_kernel_spmd(nc7, ims7, core_ids=list(range(NCORES)))
    yln_fm = np.empty((D, B, T), f32)
    for c in range(NCORES):
        for i in range(4):
            uid = c * 4 + i
            b, h = uid // 16, uid % 16
            yln_fm[h * 64:(h + 1) * 64, b, :] = np.asarray(r7.results[c][f"y_{i}"]).reshape(T, 64).T

    nc8 = _get(build_p8)
    woTs = _swz2(inputs["L1_attn_Wo"], 8)
    wss = _swz2(inputs["L1_ffn_Ws"], 8)
    bsrow = np.asarray(inputs["L1_ffn_bs"], f32).reshape(1, 8)
    ims8 = []
    for c in range(NCORES):
        b, sl = divmod(c, 4)
        ts = slice(sl * TSL, (sl + 1) * TSL)
        ims8.append({
            "yln": np.ascontiguousarray(yln_fm[:, b, ts]),
            "gkm": np.ascontiguousarray(km["g"][:, b, ts]),
            "x2r": np.ascontiguousarray(x2[b, ts].T),
            "woTs": woTs, "wss": wss, "bsrow": bsrow,
        })
    r8 = run_bass_kernel_spmd(nc8, ims8, core_ids=list(range(NCORES)))
    x3 = np.empty((B, T, D), f32)
    probs = np.empty((B * T, 8), f32)
    for c in range(NCORES):
        b, sl = divmod(c, 4)
        ts = slice(sl * TSL, (sl + 1) * TSL)
        x3[b, ts] = np.asarray(r8.results[c]["x3T"]).reshape(D, TSL).T
        pr = np.asarray(r8.results[c]["probs"]).reshape(128, TSL // 128, 8)
        probs[b * T + sl * TSL: b * T + (sl + 1) * TSL] = pr.transpose(1, 0, 2).reshape(TSL, 8)

    nc9 = _get(build_p9)
    x3tm = np.ascontiguousarray(x3.reshape(B * T, D))
    jj3, tt3 = np.meshgrid(np.arange(128), np.arange(128), indexing="ij")
    shared9 = {
        "x3tm": x3tm,
        "tokid9": np.arange(B * T, dtype=f32).reshape(128, 256),
        "trilow": (jj3 < tt3).astype(f32),
        "identb": ident,
        "bits31": np.tile((1 << (30 - np.arange(31, dtype=np.uint32))).astype(np.uint32), (128, 1)),
    }
    W1 = np.asarray(inputs["L1_ffn_W1"], f32)
    W2 = np.asarray(inputs["L1_ffn_W2"], f32)
    b1 = np.asarray(inputs["L1_ffn_b1"], f32)
    b2 = np.asarray(inputs["L1_ffn_b2"], f32)
    ims9 = []
    for e in range(NCORES):
        im = dict(shared9)
        im["pexp"] = np.ascontiguousarray(probs[:, e].reshape(128, 256))
        im["wi9"] = _swz2(W1[e], 8)
        im["wo9"] = _swz2(W2[e], 32)
        im["b1row"] = b1[e].reshape(1, 4096)
        im["b2row"] = b2[e].reshape(1, 1024)
        ims9.append(im)
    r9 = run_bass_kernel_spmd(nc9, ims9, core_ids=list(range(NCORES)))
    ebufs = [np.asarray(r9.results[e]["ebuf"]).reshape(B * T, D) for e in range(NCORES)]
    emasks = [np.asarray(r9.results[e]["emask"]).reshape(-1) for e in range(NCORES)]

    nc10 = _get(build_p10)
    ims10 = []
    for c in range(NCORES):
        b, sl = divmod(c, 4)
        g0 = b * T + sl * TSL
        im = {"x3tm_sl": np.ascontiguousarray(x3tm[g0:g0 + TSL])}
        mm8v = np.zeros((128, 8 * 32), f32)
        for e in range(8):
            im[f"eb{e}"] = np.ascontiguousarray(ebufs[e][g0:g0 + TSL])
            mm8v[:, e * 32:(e + 1) * 32] = emasks[e][g0:g0 + TSL].reshape(32, 128).T
        im["mm8"] = mm8v
        ims10.append(im)
    r10 = run_bass_kernel_spmd(nc10, ims10, core_ids=list(range(NCORES)))
    x4 = np.empty((B, T, D), f32)
    for c in range(NCORES):
        b, sl = divmod(c, 4)
        x4[b, sl * TSL:(sl + 1) * TSL] = np.asarray(r10.results[c]["x4tm"]).reshape(TSL, D)
    return x4

